# version 13
# baseline (speedup 1.0000x reference)
import numpy as np
import ml_dtypes
import concourse.bass as bass
import concourse.mybir as mybir
from concourse.bass_utils import run_bass_kernel_spmd
from contextlib import ExitStack

F32 = mybir.dt.float32
BF16 = mybir.dt.bfloat16
AX = mybir.AxisListType
ALU = mybir.AluOpType
AF = mybir.ActivationFunctionType

B, T, Dm = 16, 2048, 2048
NH, NKV, HD = 16, 4, 128
E, II, IS = 8, 1024, 4096
NCORES = 8
SEQ = B // NCORES
NG = 4
GT = T // NG
NTT = T // 128
NC = Dm // 128
NIS = IS // 128
NIE = II // 128
BT = 1024
NBLK = (SEQ * T) // BT
SCALE = 1.0 / np.sqrt(HD)
NEG = -1.0e9

ENGINES = ("tensor", "vector", "scalar", "gpsimd", "sync")


class Sm:
    __slots__ = ("h", "n", "name")

    def __init__(self, h, name):
        self.h = h
        self.n = 0
        self.name = name


class Plan:
    def __init__(self):
        self.q = {e: [] for e in ENGINES}
        self.maxw = {e: {} for e in ENGINES}
        self.last_inc = {e: None for e in ENGINES}

    def op(self, eng, fn, waits=(), inc=None, k=1):
        ws = []
        seen = self.maxw[eng]
        for s, v in waits:
            if v <= 0:
                continue
            if seen.get(id(s), -1) >= v:
                continue
            seen[id(s)] = v
            ws.append((s, v))
        tgt = None
        if inc is not None:
            inc.n += k
            tgt = (inc, k)
            self.last_inc[eng] = (inc, inc.n)
        self.q[eng].append((fn, ws, tgt))

    def fence(self, engs=ENGINES):
        return [self.last_inc[e] for e in engs if self.last_inc[e] is not None]


def dma(plan, q, out, in_, sem, waits=(), transpose=False, slow=False):
    def fn(e, o=out, i=in_, t=transpose, sl=slow):
        return e.dma_start(out=o, in_=i, transpose=t, allow_slow_non_contiguous=sl)
    plan.op(q, fn, waits, sem, 16)


def build_program(debug_x1=False, a2_depth=2):
    nc = bass.Bass("TRN2", target_bir_lowering=False)
    p = Plan()

    xin = nc.dram_tensor("x", [SEQ, T, Dm], F32, kind="ExternalInput").ap()
    wq = nc.dram_tensor("wq", [Dm, Dm], F32, kind="ExternalInput").ap()
    wk = nc.dram_tensor("wk", [Dm, NKV * HD], F32, kind="ExternalInput").ap()
    wv = nc.dram_tensor("wv", [Dm, NKV * HD], F32, kind="ExternalInput").ap()
    wo = nc.dram_tensor("wo", [Dm, Dm], F32, kind="ExternalInput").ap()
    wrouter = nc.dram_tensor("wrouter", [E, Dm], F32, kind="ExternalInput").ap()
    wg_in = nc.dram_tensor("wg", [E * II, Dm], F32, kind="ExternalInput").ap()
    wu_in = nc.dram_tensor("wu", [E * II, Dm], F32, kind="ExternalInput").ap()
    wd_in = nc.dram_tensor("wd", [E * Dm, II], F32, kind="ExternalInput").ap()
    swg_in = nc.dram_tensor("swg", [IS, Dm], F32, kind="ExternalInput").ap()
    swu_in = nc.dram_tensor("swu", [IS, Dm], F32, kind="ExternalInput").ap()
    swd_in = nc.dram_tensor("swd", [Dm, IS], F32, kind="ExternalInput").ap()
    gatew = nc.dram_tensor("gatew", [1, Dm], F32, kind="ExternalInput").ap()
    cos_in = nc.dram_tensor("cosT", [HD, T], F32, kind="ExternalInput").ap()
    sin_in = nc.dram_tensor("sinT", [HD, T], F32, kind="ExternalInput").ap()
    cosq_in = nc.dram_tensor("cosQ", [HD, T], F32, kind="ExternalInput").ap()
    sinq_in = nc.dram_tensor("sinQ", [HD, T], F32, kind="ExternalInput").ap()
    out = nc.dram_tensor("out", [SEQ, T, Dm], F32, kind="ExternalOutput").ap()
    x1dump = nc.dram_tensor("x1dump", [SEQ, T, Dm], BF16, kind="ExternalOutput").ap() if debug_x1 else None

    wkb = nc.dram_tensor("wkb", [Dm, NKV * HD], BF16, kind="Internal").ap()
    wvb = nc.dram_tensor("wvb", [Dm, NKV * HD], BF16, kind="Internal").ap()
    wqb = nc.dram_tensor("wqb", [Dm, Dm], BF16, kind="Internal").ap()
    wob = nc.dram_tensor("wob", [Dm, Dm], BF16, kind="Internal").ap()
    sgn = nc.dram_tensor("sgn", [IS, Dm], BF16, kind="Internal").ap()
    sun = nc.dram_tensor("sun", [IS, Dm], BF16, kind="Internal").ap()
    wgn = nc.dram_tensor("wgn", [E * II, Dm], BF16, kind="Internal").ap()
    wun = nc.dram_tensor("wun", [E * II, Dm], BF16, kind="Internal").ap()
    sdn = nc.dram_tensor("sdn", [Dm, IS], BF16, kind="Internal").ap()
    wdn = nc.dram_tensor("wdn", [E * Dm, II], BF16, kind="Internal").ap()
    x1tm_s = nc.dram_tensor("x1tm_s", [SEQ, T, Dm], BF16, kind="Internal").ap()

    st = ExitStack()

    def sem(name):
        return Sm(st.enter_context(nc.semaphore(name)), name)

    with st:
        s_tbl = sem("tbl")

        ident = st.enter_context(nc.sbuf_tensor("ident", [128, 128], BF16))
        dmask = st.enter_context(nc.sbuf_tensor("dmask", [128, 128], BF16))
        sel = st.enter_context(nc.sbuf_tensor("sel", [E, E * 128], BF16))
        ones1 = st.enter_context(nc.sbuf_tensor("ones1", [1, 128], BF16))
        wrt = st.enter_context(nc.sbuf_tensor("wrt", [128, NC * E], BF16))
        gwt = st.enter_context(nc.sbuf_tensor("gwt", [128, NC], BF16))
        eps_col = st.enter_context(nc.sbuf_tensor("eps_col", [128, 1], F32))

        def gen_consts(gp):
            gp.memset(ident[:], 0.0)
            gp.affine_select(out=ident[:], in_=ident[:], compare_op=ALU.not_equal,
                             fill=1.0, base=0, pattern=[[-1, 128]], channel_multiplier=1)
            gp.memset(dmask[:], 0.0)
            gp.affine_select(out=dmask[:], in_=dmask[:], compare_op=ALU.is_ge,
                             fill=NEG, base=0, pattern=[[-1, 128]], channel_multiplier=1)
            gp.memset(sel[:], 0.0)
            for e_ in range(E):
                gp.affine_select(out=sel[:, e_ * 128:(e_ + 1) * 128], in_=sel[:, e_ * 128:(e_ + 1) * 128],
                                 compare_op=ALU.not_equal, fill=1.0, base=-e_,
                                 pattern=[[0, 128]], channel_multiplier=1)
            gp.memset(ones1[:], 1.0)
            return gp.memset(eps_col[:], 1e-5)

        p.op("gpsimd", gen_consts, inc=s_tbl)
        for c in range(NC):
            dma(p, "gpsimd", wrt[:, c * E:(c + 1) * E],
                wrouter[:, c * 128:(c + 1) * 128].rearrange("e p -> p e"), s_tbl, slow=True)
            dma(p, "gpsimd", gwt[:, c:c + 1],
                gatew[:, c * 128:(c + 1) * 128].rearrange("a p -> p a"), s_tbl, slow=True)

        s_pc = sem("pc")
        s_pck = sem("pck")
        s_pcv = sem("pcv")
        s_pcq = sem("pcq")
        s_pco = sem("pco")
        prep_jobs = []

        def add_cast(src2d, dst2d, nrows, rows_per, psem=None):
            for r0 in range(0, nrows, rows_per):
                prep_jobs.append((src2d[r0:r0 + rows_per, :], dst2d[r0:r0 + rows_per, :], psem or s_pc))

        add_cast(wk, wkb, Dm, 512, s_pck)
        N_WK = 4
        add_cast(wv, wvb, Dm, 512, s_pcv)
        N_WV = 4
        add_cast(wq, wqb, Dm, 256, s_pcq)
        N_WQ = 8
        add_cast(wo, wob, Dm, 256, s_pco)
        N_ATTN_PREP = 8
        add_cast(swg_in, sgn, IS, 256)
        add_cast(swu_in, sun, IS, 256)
        add_cast(wg_in, wgn, E * II, 256)
        add_cast(wu_in, wun, E * II, 256)
        add_cast(swd_in, sdn, Dm, 128)
        add_cast(wd_in, wdn, E * Dm, 512)
        NPREP = len(prep_jobs) - 24
        NPREP_ALL = len(prep_jobs)

        prep_state = {"i": 0}

        def emit_prep(njobs):
            for _ in range(njobs):
                i = prep_state["i"]
                if i >= NPREP_ALL:
                    return
                prep_state["i"] += 1
                src, dst, psem = prep_jobs[i]
                dma(p, "gpsimd", dst, src, psem)

        ast = ExitStack()
        with ast:
            cosT = ast.enter_context(nc.sbuf_tensor("cos_sb", [128, T], BF16))
            sinT = ast.enter_context(nc.sbuf_tensor("sin_sb", [128, T], BF16))
            cosQ = ast.enter_context(nc.sbuf_tensor("cosq_sb", [128, T], BF16))
            sinQ = ast.enter_context(nc.sbuf_tensor("sinq_sb", [128, T], BF16))
            k_rope = ast.enter_context(nc.sbuf_tensor("k_rope", [128, NKV * T], BF16))
            v_tm = ast.enter_context(nc.sbuf_tensor("v_tm", [128, NTT * 512], BF16))
            q_rope = ast.enter_context(nc.sbuf_tensor("q_rope", [128, NC * GT], BF16))
            xattn = ast.enter_context(nc.sbuf_tensor("xattn", [128, NC * GT], BF16))
            xofm = ast.enter_context(nc.sbuf_tensor("xofm", [128, NC * GT], BF16))
            xstg = ast.enter_context(nc.sbuf_tensor("xstg", [128, Dm], BF16))
            xtmp = ast.enter_context(nc.sbuf_tensor("xtmp", [128, Dm], BF16))
            wsl = [ast.enter_context(nc.sbuf_tensor(f"wsl{i}", [128, NC * 512], BF16)) for i in range(2)]
            rraw = [ast.enter_context(nc.sbuf_tensor(f"rraw{i}", [128, GT], F32)) for i in range(2)]
            rsh = [ast.enter_context(nc.sbuf_tensor(f"rsh{i}", [128, GT], F32)) for i in range(2)]
            prow = ast.enter_context(nc.sbuf_tensor("prow", [128, T], BF16))
            atb = ast.enter_context(nc.sbuf_tensor("atb", [128, T], BF16))
            ygrp = ast.enter_context(nc.sbuf_tensor("ygrp", [128, NH * 512], BF16))
            xres = ast.enter_context(nc.sbuf_tensor("xres", [128, Dm], F32))
            pre = ast.enter_context(nc.sbuf_tensor("pre", [128, Dm], F32))
            attn_tm = ast.enter_context(nc.sbuf_tensor("attn_tm", [128, Dm], BF16))
            x1st = ast.enter_context(nc.sbuf_tensor("x1st", [128, Dm], BF16))
            col_m = ast.enter_context(nc.sbuf_tensor("col_m", [128, 8], F32))
            col_b = ast.enter_context(nc.sbuf_tensor("col_b", [128, 2], F32))
            col_se = ast.enter_context(nc.sbuf_tensor("col_se", [128, 2], F32))
            col_rc = ast.enter_context(nc.sbuf_tensor("col_rc", [128, 2], F32))
            ln_ns = ast.enter_context(nc.sbuf_tensor("ln_ns", [128, 1], F32))
            ln_sq = ast.enter_context(nc.sbuf_tensor("ln_sq", [128, 1], F32))
            ln_nm = ast.enter_context(nc.sbuf_tensor("ln_nm", [128, 1], F32))
            ln_m2 = ast.enter_context(nc.sbuf_tensor("ln_m2", [128, 1], F32))
            ln_var = ast.enter_context(nc.sbuf_tensor("ln_var", [128, 1], F32))
            ln_std = ast.enter_context(nc.sbuf_tensor("ln_std", [128, 1], F32))
            ln_rs = ast.enter_context(nc.sbuf_tensor("ln_rs", [128, 1], F32))

            ps_row = ast.enter_context(nc.psum_tensor("ps_row", [128, T], F32))
            ps_y = [ast.enter_context(nc.psum_tensor(f"ps_y{i}", [128, 512], F32)) for i in range(2)]
            ps_a = [ast.enter_context(nc.psum_tensor(f"ps_a{i}", [128, 512], F32)) for i in range(2)]

            s_tld = sem("tld")
            s_xld = sem("xld")
            s_xtr0 = sem("xtr0")
            s_xtr = sem("xtr")
            s_wld = sem("wld")
            s_pa = sem("pa")
            s_pev = sem("pev")
            s_rsh = sem("rsh")
            s_rt1 = sem("rt1")
            s_rt2 = sem("rt2")
            s_kq = sem("kq")
            s_v = sem("v")
            s_scr = sem("scr")
            s_msk = sem("msk")
            s_max = sem("max")
            s_bias = sem("bias")
            s_exp = sem("exp")
            s_rcp = sem("rcp")
            s_psc = sem("psc")
            s_at = sem("at")
            s_at2 = sem("at2")
            s_av = sem("av")
            s_yev = sem("yev")
            s_oev = sem("oev")
            s_otr = sem("otr")
            s_xr = sem("xr")
            s_lpre = sem("lpre")
            s_lns = sem("lns")
            s_lsq = sem("lsq")
            s_lt1 = sem("lt1")
            s_lt2 = sem("lt2")
            s_lt3 = sem("lt3")
            s_lst = sem("lst")
            s_lrs = sem("lrs")
            s_x1 = sem("x1")
            s_x1w = sem("x1w")

            dma(p, "gpsimd", cosT[:], cos_in, s_tld)
            dma(p, "gpsimd", sinT[:], sin_in, s_tld)
            dma(p, "gpsimd", cosQ[:], cosq_in, s_tld)
            dma(p, "gpsimd", sinQ[:], sinq_in, s_tld)

            xattn_v = xattn[:].rearrange("p (c t) -> p c t", c=NC)
            xofm_t4 = xofm[:].rearrange("p (t c f) -> p t c f", t=NG, c=NC)

            wsl_user = [0, 0]
            slab_ctr = [0]
            rope_user = [0, 0]
            rope_ctr = [0]
            row_user = [0, 0]
            prow_user = [(0, 0), (0, 0)]
            atb_user = [0, 0]
            se_user = [0, 0]
            y_user = [0, 0]
            a2_count = [0]
            yq_count = [0]
            xcopy_gate = [0]

            def rope_chunk(seq, g, psum, dst_col_ap, ctab, stab):
                sl = rope_ctr[0] % 2
                rope_ctr[0] += 1
                rw, rs = rraw[sl], rsh[sl]
                p.op("scalar", lambda e, ps=psum, rw=rw: e.copy(out=rw[:], in_=ps[:]),
                     waits=[(s_pa, s_pa.n), (s_kq, rope_user[sl])], inc=s_pev)
                n_ev = s_pev.n
                dma(p, "sync", rs[0:64, :], rw[64:128, :], s_rsh,
                    waits=[(s_pev, n_ev), (s_kq, rope_user[sl])])
                dma(p, "sync", rs[64:128, :], rw[0:64, :], s_rsh, waits=[(s_pev, n_ev)])
                n_sh = s_rsh.n
                p.op("vector", lambda e, sl_=(g * GT), rw=rw: e.tensor_tensor(
                    out=rw[:], in0=rw[:], in1=ctab[:, sl_:sl_ + GT], op=ALU.mult),
                    waits=[(s_pev, n_ev), (s_rsh, n_sh), (s_tld, 64)], inc=s_rt1)
                p.op("vector", lambda e, sl_=(g * GT), rs=rs: e.tensor_tensor(
                    out=rs[:], in0=rs[:], in1=stab[:, sl_:sl_ + GT], op=ALU.mult),
                    waits=[(s_rsh, n_sh), (s_tld, 64)], inc=s_rt2)
                p.op("vector", lambda e, d=dst_col_ap, rw=rw, rs=rs: e.tensor_tensor(
                    out=d, in0=rw[:], in1=rs[:], op=ALU.add),
                    waits=[(s_rt1, s_rt1.n), (s_rt2, s_rt2.n), (s_scr, s_scr.n)], inc=s_kq)
                rope_user[sl] = s_kq.n

            def load_slab(src_fn, ncols, psem, prep_n):
                slot = slab_ctr[0] % 2
                slab_ctr[0] += 1
                for c in range(NC):
                    dma(p, "gpsimd", wsl[slot][:, c * 512:c * 512 + ncols], src_fn(c), s_wld,
                        waits=[(s_pa, wsl_user[slot]), (psem, 16 * prep_n)])
                return slot, s_wld.n

            def a1_xpath(seq, g):
                for t in range(NG):
                    tok0 = g * GT + t * 128
                    dma(p, "gpsimd", xstg[:], xin[seq, tok0:tok0 + 128, :], s_xld,
                        waits=[(s_xtr0, s_xtr0.n)])
                    dma(p, "sync", xtmp[:].rearrange("p (c f) -> p c f", c=NC),
                        xstg[:].rearrange("p (c f) -> p c f", c=NC), s_xtr0,
                        waits=[(s_xld, s_xld.n), (s_xtr, s_xtr.n)], transpose=True)
                    p.op("scalar", lambda e, t=t: e.copy(out=xattn_v[:, :, t * 128:(t + 1) * 128],
                                                         in_=xtmp[:].rearrange("p (c f) -> p c f", c=NC)),
                         waits=[(s_xtr0, s_xtr0.n), (s_pa, xcopy_gate[0])], inc=s_xtr)

            def stage_a1(seq, g):
                n_xtr = s_xtr.n

                slot, nw = load_slab(lambda c: wkb[c * 128:(c + 1) * 128, :], 512, s_pck, N_WK)
                for o in range(NKV):
                    pb = ps_a[o % 2]
                    def mmk(e, o=o, pb=pb, slot=slot):
                        ins = None
                        for c in range(NC):
                            ins = e.matmul(pb[:], wsl[slot][:, c * 512 + o * 128: c * 512 + (o + 1) * 128],
                                           xattn_v[:, c, :], start=(c == 0), stop=(c == NC - 1))
                        return ins
                    p.op("tensor", mmk,
                         waits=[(s_wld, nw), (s_xtr, n_xtr), (s_pev, s_pev.n), (s_tbl, 1)], inc=s_pa)
                    rope_chunk(seq, g, pb, k_rope[:, o * T + g * GT: o * T + (g + 1) * GT], cosT, sinT)
                wsl_user[slot] = s_pa.n
                slot, nw = load_slab(lambda c: wvb[c * 128:(c + 1) * 128, :], 512, s_pcv, N_WV)
                for t in range(NG):
                    pb = ps_a[t % 2]
                    def mmv(e, t=t, pb=pb, slot=slot):
                        ins = None
                        for c in range(NC):
                            ins = e.matmul(pb[:], xattn_v[:, c, t * 128:(t + 1) * 128],
                                           wsl[slot][:, c * 512:(c + 1) * 512],
                                           start=(c == 0), stop=(c == NC - 1))
                        return ins
                    p.op("tensor", mmv, waits=[(s_wld, nw), (s_pev, s_pev.n)], inc=s_pa)
                    tt = g * NG + t
                    p.op("scalar", lambda e, pb=pb, tt=tt: e.copy(
                        out=v_tm[:, tt * 512:(tt + 1) * 512], in_=pb[:]),
                        waits=[(s_pa, s_pa.n), (s_av, s_av.n)], inc=s_v)
                wsl_user[slot] = s_pa.n
                for qtr in range(4):
                    slot, nw = load_slab(lambda c, qtr=qtr: wqb[c * 128:(c + 1) * 128,
                                                               qtr * 512:(qtr + 1) * 512], 512, s_pcq, N_WQ)
                    for oo in range(4):
                        o = qtr * 4 + oo
                        pb = ps_a[o % 2]
                        def mmq(e, oo=oo, slot=slot, pb=pb):
                            ins = None
                            for c in range(NC):
                                ins = e.matmul(pb[:], wsl[slot][:, c * 512 + oo * 128: c * 512 + (oo + 1) * 128],
                                               xattn_v[:, c, :], start=(c == 0), stop=(c == NC - 1))
                            return ins
                        p.op("tensor", mmq, waits=[(s_wld, nw), (s_pev, s_pev.n)], inc=s_pa)
                        rope_chunk(seq, g, pb, q_rope[:, o * GT:(o + 1) * GT], cosQ, sinQ)
                    wsl_user[slot] = s_pa.n
                xcopy_gate[0] = s_pa.n

            def stage_a2(seq, i, h):
                k = a2_count[0]
                a2_count[0] += 1
                g = h // 4
                w = (i // 4) * 512 + (i % 4 + 1) * 128
                nt = w // 128
                if i < 8:
                    halves = (k % 2,)
                    off = (k % 2) * 1024
                else:
                    halves = (0, 1)
                    off = 0
                sl = halves[0] if len(halves) == 1 else 0
                cs = sl if len(halves) == 1 else 0
                own = halves
                row_wait = max(row_user[h_] for h_ in own)
                prow_wait1 = max(prow_user[h_][0] for h_ in own)
                prow_wait2 = max(prow_user[h_][1] for h_ in own)
                atb_wait = max(atb_user[h_] for h_ in own)
                se_wait = max(se_user[h_] for h_ in own)
                pr_ap = prow[:, off:off + w]
                at_ap = atb[:, off:off + w]
                cm = col_m[:, cs * 4:cs * 4 + 4]
                cb = col_b[:, cs:cs + 1]
                cse = col_se[:, cs:cs + 1]
                crc = col_rc[:, cs:cs + 1]

                nblk = (w + 511) // 512
                for kb in range(nblk):
                    bw = min(512, w - kb * 512)
                    def mblk(e, i=i, h=h, g=g, off=off, kb=kb, bw=bw, last=(kb == nblk - 1)):
                        ins = e.matmul(ps_row[:, off + kb * 512: off + kb * 512 + bw],
                                       q_rope[:, h * GT + (i % 4) * 128: h * GT + (i % 4 + 1) * 128],
                                       k_rope[:, g * T + kb * 512: g * T + kb * 512 + bw],
                                       start=True, stop=not last)
                        if last:
                            ins = e.matmul(ps_row[:, off + w - 128: off + w], ident[:], dmask[:],
                                           start=False, stop=True)
                        return ins
                    p.op("tensor", mblk,
                         waits=([(s_kq, (seq * NG + i // 4 + 1) * 20), (s_exp, row_wait),
                                 (s_tbl, 1)] if kb == 0 else []),
                         inc=s_scr)
                    n_scr = s_scr.n
                    p.op("vector", lambda e, off=off, kb=kb, bw=bw, cm=cm: e.reduce_max(
                        out=cm[:, kb:kb + 1], in_=ps_row[:, off + kb * 512: off + kb * 512 + bw], axis=AX.X),
                        waits=[(s_scr, n_scr), (s_bias, s_bias.n)], inc=s_max)
                p.op("vector", lambda e, cm=cm, cb=cb, nblk=nblk: e.reduce_max(
                    out=cb, in_=cm[:, 0:nblk], axis=AX.X, negate=True),
                    waits=[(s_max, s_max.n), (s_exp, s_exp.n)], inc=s_bias)
                p.op("scalar", lambda e, w=w, off=off, pr=pr_ap, cb=cb, cse=cse: e.activation(
                    out=pr, in_=ps_row[:, off:off + w], func=AF.Exp,
                    bias=cb, scale=1.0, accum_out=cse),
                    waits=[(s_bias, s_bias.n), (s_at, prow_wait1), (s_at2, prow_wait2),
                           (s_rcp, se_wait)])
                p.op("scalar", lambda e, cse=cse: e.copy(out=cse, in_=cse), waits=[], inc=s_exp)
                for h_ in own:
                    row_user[h_] = s_exp.n
                p.op("vector", lambda e, cse=cse, crc=crc: e.reciprocal(out=crc, in_=cse),
                     waits=[(s_exp, s_exp.n)], inc=s_rcp)
                for h_ in own:
                    se_user[h_] = s_rcp.n
                p.op("vector", lambda e, pr=pr_ap, crc=crc: e.tensor_scalar(
                    out=pr, in0=pr, scalar1=crc, scalar2=None, op0=ALU.mult),
                    waits=[(s_rcp, s_rcp.n), (s_at, prow_wait1), (s_at2, prow_wait2)], inc=s_psc)
                at_sem = s_at
                dma(p, "sync", at_ap.rearrange("p (c f) -> p c f", c=nt),
                    pr_ap.rearrange("p (c f) -> p c f", c=nt), at_sem,
                    waits=[(s_psc, s_psc.n), (s_av, atb_wait)], transpose=True)
                for h_ in own:
                    prow_user[h_] = (s_at.n, s_at2.n)
                n_at_here = (at_sem, at_sem.n)

                def emit_av(seq=seq, i=i, h=h, g=g, nt=nt, off=off, own=own, n_at_here=n_at_here):
                    qd = yq_count[0]
                    pyb = ps_y[qd % 2]
                    def mmav(e, g=g, h=h, nt=nt, off=off, pyb=pyb):
                        ins = None
                        for j in range(nt):
                            ins = e.matmul(pyb[:, (h % 4) * 128:(h % 4 + 1) * 128],
                                           v_tm[:, j * 512 + g * 128: j * 512 + (g + 1) * 128],
                                           atb[:, off + j * 128: off + (j + 1) * 128],
                                           start=(j == 0), stop=(j == nt - 1))
                        return ins
                    p.op("tensor", mmav,
                         waits=[(n_at_here[0], n_at_here[1]), (s_v, (seq * NG + i // 4 + 1) * NG),
                                (s_yev, y_user[qd % 2])],
                         inc=s_av)
                    for h_ in own:
                        atb_user[h_] = s_av.n
                    if h % 4 == 3:
                        hq = h - 3
                        def yev(e, i=i, hq=hq, pyb=pyb):
                            og = ygrp[:].rearrange("p (hh t) -> p hh t", hh=NH)
                            return e.copy(out=og[:, hq:hq + 4, (i % 4) * 128:(i % 4 + 1) * 128],
                                          in_=pyb[:].rearrange("p (hh t) -> p hh t", hh=4))
                        p.op("scalar", yev, waits=[(s_av, s_av.n), (s_pa, s_pa.n)], inc=s_yev)
                        y_user[qd % 2] = s_yev.n
                        yq_count[0] += 1
                return emit_av

            def stage_a3(seq, g):
                for qtr in range(4):
                    slot, nw = load_slab(lambda c, qtr=qtr: wob[c * 128:(c + 1) * 128,
                                                               qtr * 512:(qtr + 1) * 512], 512, s_pco, N_ATTN_PREP)
                    for ff in range(4):
                        f = qtr * 4 + ff
                        pb = ps_a[f % 2]
                        def mmo(e, ff=ff, slot=slot, pb=pb):
                            ins = None
                            for c in range(NC):
                                ins = e.matmul(pb[:], wsl[slot][:, c * 512 + ff * 128: c * 512 + (ff + 1) * 128],
                                               ygrp[:, c * 512:(c + 1) * 512],
                                               start=(c == 0), stop=(c == NC - 1))
                            return ins
                        p.op("tensor", mmo,
                             waits=[(s_wld, nw), (s_yev, yq_count[0]), (s_pev, s_pev.n)],
                             inc=s_pa)
                        p.op("scalar", lambda e, pb=pb, f=f: e.copy(
                            out=xofm_t4[:, :, f, :],
                            in_=pb[:].rearrange("p (t f) -> p t f", t=NG)),
                            waits=[(s_pa, s_pa.n), (s_otr, s_otr.n)], inc=s_oev)
                    wsl_user[slot] = s_pa.n
                for t in range(NG):
                    tok0 = g * GT + t * 128
                    dma(p, "sync", attn_tm[:].rearrange("p (c f) -> p c f", c=NC),
                        xofm_t4[:, t, :, :], s_otr,
                        waits=[(s_oev, s_oev.n), (s_lpre, s_lpre.n)], transpose=True)
                    dma(p, "sync", xres[:], xin[seq, tok0:tok0 + 128, :], s_xr,
                        waits=[(s_lsq, s_lsq.n)])
                    p.op("vector", lambda e: e.tensor_tensor(out=pre[:], in0=attn_tm[:], in1=xres[:], op=ALU.add),
                         waits=[(s_otr, s_otr.n), (s_xr, s_xr.n), (s_x1, s_x1.n)], inc=s_lpre)
                    p.op("vector", lambda e: e.reduce_sum(out=ln_ns[:], in_=pre[:], axis=AX.X, negate=True),
                         waits=[(s_lpre, s_lpre.n), (s_lt1, s_lt1.n)], inc=s_lns)
                    p.op("scalar", lambda e: e.activation(out=xres[:], in_=pre[:], func=AF.Square,
                                                          accum_out=ln_sq[:]),
                         waits=[(s_lpre, s_lpre.n), (s_lt3, s_lt3.n)])
                    p.op("scalar", lambda e: e.copy(out=ln_sq[:], in_=ln_sq[:]), waits=[], inc=s_lsq)
                    p.op("vector", lambda e: e.tensor_scalar(out=ln_nm[:], in0=ln_ns[:], scalar1=1.0 / Dm,
                                                             scalar2=None, op0=ALU.mult),
                         waits=[(s_lns, s_lns.n), (s_x1, s_x1.n)], inc=s_lt1)
                    p.op("vector", lambda e: e.tensor_tensor(out=ln_m2[:], in0=ln_nm[:], in1=ln_nm[:], op=ALU.mult),
                         waits=[(s_lt1, s_lt1.n), (s_lt3, s_lt3.n)], inc=s_lt2)
                    p.op("vector", lambda e: e.tensor_scalar(out=ln_var[:], in0=ln_sq[:], scalar1=1.0 / Dm,
                                                             scalar2=ln_m2[:], op0=ALU.mult, op1=ALU.subtract),
                         waits=[(s_lsq, s_lsq.n), (s_lt2, s_lt2.n), (s_lst, s_lst.n)], inc=s_lt3)
                    p.op("scalar", lambda e: e.activation(out=ln_std[:], in_=ln_var[:], func=AF.Sqrt, bias=eps_col[:]),
                         waits=[(s_lt3, s_lt3.n), (s_lrs, s_lrs.n)], inc=s_lst)
                    p.op("vector", lambda e: e.reciprocal(out=ln_rs[:], in_=ln_std[:]),
                         waits=[(s_lst, s_lst.n), (s_x1, s_x1.n)], inc=s_lrs)
                    p.op("vector", lambda e: e.tensor_scalar(out=x1st[:], in0=pre[:], scalar1=ln_nm[:],
                                                             scalar2=ln_rs[:], op0=ALU.add, op1=ALU.mult),
                         waits=[(s_lrs, s_lrs.n), (s_x1w, s_x1w.n)], inc=s_x1)
                    dma(p, "sync", x1tm_s[seq, tok0:tok0 + 128, :], x1st[:], s_x1w,
                        waits=[(s_x1, s_x1.n)])

            emit_prep(24)
            a1_xpath(0, 0)
            for seq in range(SEQ):
                for g in range(NG):
                    stage_a1(seq, g)
                    emit_prep(3)
                    pend = []
                    for it in range(NG):
                        i = g * NG + it
                        for h in range(NH):
                            limit = a2_depth if i < 8 else min(a2_depth, 1)
                            while pend and len(pend) >= limit:
                                pend.pop(0)()
                            pend.append(stage_a2(seq, i, h))
                            emit_prep(1)
                        if it == 1:
                            if g + 1 < NG:
                                a1_xpath(seq, g + 1)
                            elif seq + 1 < SEQ:
                                a1_xpath(seq + 1, 0)
                    for f_ in pend:
                        f_()
                    stage_a3(seq, g)
                    emit_prep(6)
            emit_prep(10 ** 9)
            n_x1w_all = s_x1w.n
            a_fence = p.fence()

        bst = ExitStack()
        with bst:
            x1f = bst.enter_context(nc.sbuf_tensor("x1f", [128, NC * BT], BF16))
            x1ldstg = bst.enter_context(nc.sbuf_tensor("x1ldstg", [128, Dm], BF16))
            x1resstg = bst.enter_context(nc.sbuf_tensor("x1resstg", [128, Dm], BF16))
            xtmpb = bst.enter_context(nc.sbuf_tensor("xtmpb", [128, Dm], BF16))
            acc = bst.enter_context(nc.sbuf_tensor("acc", [128, NC * BT], F32))
            sfm = bst.enter_context(nc.sbuf_tensor("sfm", [128, 8 * BT], BF16))
            gbuf = [bst.enter_context(nc.sbuf_tensor(f"gbuf{i}", [128, Dm], BF16)) for i in range(2)]
            ubuf = [bst.enter_context(nc.sbuf_tensor(f"ubuf{i}", [128, Dm], BF16)) for i in range(2)]
            dbuf = [bst.enter_context(nc.sbuf_tensor(f"dbuf{i}", [128, 1024], BF16)) for i in range(2)]
            cbs = bst.enter_context(nc.sbuf_tensor("cbs", [128, BT], F32))
            sgb = bst.enter_context(nc.sbuf_tensor("sgb", [128, BT], F32))
            sgrow = bst.enter_context(nc.sbuf_tensor("sgrow", [1, BT], F32))
            sgrow_b = bst.enter_context(nc.sbuf_tensor("sgrow_b", [1, BT], BF16))
            gs = bst.enter_context(nc.sbuf_tensor("gs", [128, BT], BF16))
            usc = bst.enter_context(nc.sbuf_tensor("usc", [128, BT], BF16))
            rp = bst.enter_context(nc.sbuf_tensor("rp", [128, E], F32))
            rm1 = bst.enter_context(nc.sbuf_tensor("rm1", [128, 1], F32))
            rmask = bst.enter_context(nc.sbuf_tensor("rmask", [128, E], F32))
            rp2 = bst.enter_context(nc.sbuf_tensor("rp2", [128, E], F32))
            rm2 = bst.enter_context(nc.sbuf_tensor("rm2", [128, 1], F32))
            rge = bst.enter_context(nc.sbuf_tensor("rge", [128, E], F32))
            rcr = bst.enter_context(nc.sbuf_tensor("rcr", [128, E], F32))
            rden = bst.enter_context(nc.sbuf_tensor("rden", [128, 1], F32))
            rrc = bst.enter_context(nc.sbuf_tensor("rrc", [128, 1], F32))
            rcomb = bst.enter_context(nc.sbuf_tensor("rcomb", [128, E], BF16))
            combfm = bst.enter_context(nc.sbuf_tensor("combfm", [E, BT], BF16))
            moetm = [bst.enter_context(nc.sbuf_tensor(f"moetm{i}", [128, Dm], BF16)) for i in range(4)]
            pre2 = bst.enter_context(nc.sbuf_tensor("pre2", [128, Dm], F32))
            x2st = bst.enter_context(nc.sbuf_tensor("x2st", [128, Dm], F32))
            b_ns = bst.enter_context(nc.sbuf_tensor("b_ns", [128, 1], F32))
            b_sq = bst.enter_context(nc.sbuf_tensor("b_sq", [128, 1], F32))
            b_nm = bst.enter_context(nc.sbuf_tensor("b_nm", [128, 1], F32))
            b_m2 = bst.enter_context(nc.sbuf_tensor("b_m2", [128, 1], F32))
            b_var = bst.enter_context(nc.sbuf_tensor("b_var", [128, 1], F32))
            b_std = bst.enter_context(nc.sbuf_tensor("b_std", [128, 1], F32))
            b_rs = bst.enter_context(nc.sbuf_tensor("b_rs", [128, 1], F32))

            ps_g = bst.enter_context(nc.psum_tensor("ps_g", [128, 1024], F32))
            ps_u = bst.enter_context(nc.psum_tensor("ps_u", [128, 1024], F32))
            ps_d = bst.enter_context(nc.psum_tensor("ps_d", [128, 1024], F32))
            ps_m = bst.enter_context(nc.psum_tensor("ps_m", [128, 512], F32))
            ps_rt = bst.enter_context(nc.psum_tensor("ps_rt", [E, 128], BF16))

            s_bx1 = sem("bx1")
            s_bx1t = sem("bx1t")
            s_bx1c = sem("bx1c")
            s_bwg = sem("bwg")
            s_bwd = sem("bwd")
            s_bg = sem("bg")
            s_bu = sem("bu")
            s_bsil = sem("bsil")
            s_busc = sem("busc")
            s_bgu = sem("bgu")
            s_bd = sem("bd")
            s_bacc = sem("bacc")
            s_brt = sem("brt")
            s_brx = sem("brx")
            s_bcb = sem("bcb")
            s_bsg = sem("bsg")
            s_bcast = sem("bcast")
            s_btr = sem("btr")
            s_bx1r = sem("bx1r")
            s_bpre = sem("bpre")
            s_bns = sem("bns")
            s_bsq = sem("bsq")
            s_bt1 = sem("bt1")
            s_bt2 = sem("bt2")
            s_bt3 = sem("bt3")
            s_bst_ = sem("bst")
            s_brs = sem("brs")
            s_bx2 = sem("bx2")
            s_out = sem("sout")

            x1f_v = x1f[:].rearrange("p (c t) -> p c t", c=NC)
            acc_v = acc[:].rearrange("p (c t) -> p c t", c=NC)
            sfm_v = sfm[:].rearrange("p (c t) -> p c t", c=8)

            first_b = [True]
            wgu_user = [0, 0]
            wd_user = [0, 0]
            gh_user = [0, 0]
            uh_user = [0, 0]
            dh_user = [0, 0]
            ug_ctr = [0]
            dn_ctr = [0]
            mtm_user = [0, 0, 0, 0]
            btr_tile = [0] * 8
            bcast_tile = [0] * 8

            def stage_b(blk):
                seq = blk // 2
                off = (blk % 2) * BT
                fence = a_fence if first_b[0] else []
                base_x1c = s_bx1c.n
                for t in range(8):
                    tok0 = off + t * 128
                    dma(p, "sync", x1ldstg[:], x1tm_s[seq, tok0:tok0 + 128, :], s_bx1,
                        waits=[(s_x1w, n_x1w_all), (s_bx1t, s_bx1t.n)] + fence)
                    fence = []
                    dma(p, "sync", xtmpb[:].rearrange("p (c f) -> p c f", c=NC),
                        x1ldstg[:].rearrange("p (c f) -> p c f", c=NC), s_bx1t,
                        waits=[(s_bx1, s_bx1.n), (s_bx1c, s_bx1c.n)], transpose=True)
                    p.op("scalar", lambda e, t=t: e.copy(out=x1f_v[:, :, t * 128:(t + 1) * 128],
                                                         in_=xtmpb[:].rearrange("p (c f) -> p c f", c=NC)),
                         waits=[(s_bx1t, s_bx1t.n), (s_bg, s_bg.n), (s_bu, s_bu.n), (s_brt, s_brt.n),
                                (s_btr, btr_tile[t]), (s_bcast, bcast_tile[t])],
                         inc=s_bx1c)
                first_b[0] = False
                n_x1t = s_bx1c.n

                for t in range(8):
                    def mmr(e, t=t):
                        ins = None
                        for c in range(NC):
                            ins = e.matmul(ps_m[:, 0:E], x1f_v[:, c, t * 128:(t + 1) * 128],
                                           wrt[:, c * E:(c + 1) * E], start=(c == 0), stop=(c == NC - 1))
                        return ins
                    p.op("tensor", mmr, waits=[(s_bx1c, base_x1c + t + 1), (s_brx, s_brx.n),
                                               (s_tbl, s_tbl.n)], inc=s_brt)
                    p.op("vector", lambda e: e.reduce_max(out=rm1[:], in_=ps_m[:, 0:E], axis=AX.X, negate=True),
                         waits=[(s_brt, s_brt.n)], inc=s_brx)
                    p.op("scalar", lambda e: e.activation(out=rp[:], in_=ps_m[:, 0:E], func=AF.Exp, bias=rm1[:]),
                         waits=[(s_brx, s_brx.n)], inc=s_brx)
                    p.op("vector", lambda e: e.reduce_max(out=rm1[:], in_=rp[:], axis=AX.X),
                         waits=[(s_brx, s_brx.n)], inc=s_brx)
                    p.op("vector", lambda e: e.tensor_scalar(out=rmask[:], in0=rp[:], scalar1=rm1[:],
                                                             scalar2=None, op0=ALU.is_ge),
                         waits=[(s_brx, s_brx.n)], inc=s_brx)
                    p.op("vector", lambda e: e.scalar_tensor_tensor(out=rp2[:], in0=rmask[:], scalar=NEG,
                                                                    in1=rp[:], op0=ALU.mult, op1=ALU.add),
                         waits=[(s_brx, s_brx.n)], inc=s_brx)
                    p.op("vector", lambda e: e.reduce_max(out=rm2[:], in_=rp2[:], axis=AX.X),
                         waits=[(s_brx, s_brx.n)], inc=s_brx)
                    p.op("vector", lambda e: e.tensor_scalar(out=rge[:], in0=rp[:], scalar1=rm2[:],
                                                             scalar2=None, op0=ALU.is_ge),
                         waits=[(s_brx, s_brx.n)], inc=s_brx)
                    p.op("vector", lambda e: e.tensor_tensor(out=rcr[:], in0=rp[:], in1=rge[:], op=ALU.mult),
                         waits=[(s_brx, s_brx.n)], inc=s_brx)
                    p.op("vector", lambda e: e.reduce_sum(out=rden[:], in_=rcr[:], axis=AX.X),
                         waits=[(s_brx, s_brx.n)], inc=s_brx)
                    p.op("vector", lambda e: e.reciprocal(out=rrc[:], in_=rden[:]),
                         waits=[(s_brx, s_brx.n)], inc=s_brx)
                    p.op("vector", lambda e: e.tensor_scalar(out=rcomb[:], in0=rcr[:], scalar1=rrc[:],
                                                             scalar2=None, op0=ALU.mult),
                         waits=[(s_brx, s_brx.n)], inc=s_brx)
                    p.op("tensor", lambda e: e.transpose(ps_rt[:], rcomb[:], ident[:]),
                         waits=[(s_brx, s_brx.n), (s_bcb, s_bcb.n)], inc=s_brt)
                    p.op("vector", lambda e, t=t: e.tensor_copy(out=combfm[:, t * 128:(t + 1) * 128], in_=ps_rt[:]),
                         waits=[(s_brt, s_brt.n)], inc=s_brx)
                n_comb = s_brx.n

                for tg in range(2):
                    def mmsg2(e, tg=tg):
                        ins = None
                        for c in range(NC):
                            ins = e.matmul(ps_m[0:1, 0:512], gwt[:, c:c + 1],
                                           x1f_v[:, c, tg * 512:(tg + 1) * 512],
                                           start=(c == 0), stop=(c == NC - 1))
                        return ins
                    p.op("tensor", mmsg2, waits=[(s_bx1c, n_x1t), (s_bsg, s_bsg.n), (s_brx, s_brx.n)], inc=s_brt)
                    p.op("scalar", lambda e, tg=tg: e.activation(out=sgrow[:, tg * 512:(tg + 1) * 512],
                                                                 in_=ps_m[0:1, 0:512], func=AF.Sigmoid),
                         waits=[(s_brt, s_brt.n)], inc=s_bsg)
                p.op("vector", lambda e: e.tensor_copy(out=sgrow_b[:], in_=sgrow[:]),
                     waits=[(s_bsg, s_bsg.n)], inc=s_bsg)
                for tg in range(2):
                    p.op("tensor", lambda e, tg=tg: e.matmul(ps_m[:, 0:512], ones1[:],
                                                             sgrow_b[:, tg * 512:(tg + 1) * 512],
                                                             start=True, stop=True),
                         waits=[(s_bsg, s_bsg.n), (s_brx, n_comb)], inc=s_brt)
                    p.op("scalar", lambda e, tg=tg: e.copy(out=sgb[:, tg * 512:(tg + 1) * 512], in_=ps_m[:, 0:512]),
                         waits=[(s_brt, s_brt.n), (s_busc, s_busc.n)], inc=s_bsg)
                n_sgb = s_bsg.n

                def upgate(i_local, gsrc_rows, usrc_rows, scale_sb, n_scale_ready, scale_sem):
                    slot = ug_ctr[0] % 2
                    ug_ctr[0] += 1
                    dma(p, "sync", gbuf[slot][:].rearrange("p (c f) -> p c f", c=NC), gsrc_rows, s_bwg,
                        waits=[(s_bg, wgu_user[slot]), (s_pc, 16 * NPREP)], transpose=True)
                    dma(p, "sync", ubuf[slot][:].rearrange("p (c f) -> p c f", c=NC), usrc_rows, s_bwg,
                        waits=[(s_bu, wgu_user[slot])], transpose=True)
                    nw = s_bwg.n
                    gb = gbuf[slot][:].rearrange("p (c f) -> p c f", c=NC)
                    ub = ubuf[slot][:].rearrange("p (c f) -> p c f", c=NC)
                    for tg in range(2):
                        def mmg(e, gb=gb, tg=tg):
                            ins = None
                            for c in range(NC):
                                ins = e.matmul(ps_g[:, tg * 512:(tg + 1) * 512], gb[:, c, :],
                                               x1f_v[:, c, tg * 512:(tg + 1) * 512],
                                               start=(c == 0), stop=(c == NC - 1))
                            return ins
                        p.op("tensor", mmg, waits=[(s_bwg, nw), (s_bx1c, n_x1t), (s_bsil, gh_user[tg])],
                             inc=s_bg)
                        p.op("scalar", lambda e, tg=tg: e.activation(
                            out=gs[:, tg * 512:(tg + 1) * 512], in_=ps_g[:, tg * 512:(tg + 1) * 512],
                            func=AF.Silu),
                            waits=[(s_bg, s_bg.n), (s_bgu, s_bgu.n)], inc=s_bsil)
                        gh_user[tg] = s_bsil.n
                    n_bg_here = s_bg.n
                    for tg in range(2):
                        def mmu(e, ub=ub, tg=tg):
                            ins = None
                            for c in range(NC):
                                ins = e.matmul(ps_u[:, tg * 512:(tg + 1) * 512], ub[:, c, :],
                                               x1f_v[:, c, tg * 512:(tg + 1) * 512],
                                               start=(c == 0), stop=(c == NC - 1))
                            return ins
                        p.op("tensor", mmu, waits=[(s_bwg, nw), (s_busc, uh_user[tg])], inc=s_bu)
                        p.op("vector", lambda e, tg=tg, sc=scale_sb: e.tensor_tensor(
                            out=usc[:, tg * 512:(tg + 1) * 512], in0=ps_u[:, tg * 512:(tg + 1) * 512],
                            in1=sc[:, tg * 512:(tg + 1) * 512], op=ALU.mult),
                            waits=[(s_bu, s_bu.n), (scale_sem, n_scale_ready), (s_bgu, s_bgu.n)], inc=s_busc)
                        uh_user[tg] = s_busc.n
                        p.op("gpsimd", lambda e, tg=tg, il=i_local: e.tensor_tensor(
                            out=sfm_v[:, il, tg * 512:(tg + 1) * 512],
                            in0=gs[:, tg * 512:(tg + 1) * 512], in1=usc[:, tg * 512:(tg + 1) * 512],
                            op=ALU.mult),
                            waits=[(s_bsil, s_bsil.n), (s_busc, s_busc.n), (s_bd, s_bd.n)], inc=s_bgu)
                    wgu_user[slot] = n_bg_here

                def downpass(nich, dsrc_rows_fn, first_contrib):
                    ngu = s_bgu.n
                    for fc in range(NC):
                        slot = dn_ctr[0] % 2
                        dn_ctr[0] += 1
                        dma(p, "sync", dbuf[slot][:, 0:nich * 128].rearrange("p (c f) -> p c f", c=nich),
                            dsrc_rows_fn(fc), s_bwd,
                            waits=[(s_bd, wd_user[slot]), (s_pc, 16 * NPREP)], transpose=True)
                        nwd = s_bwd.n
                        db = dbuf[slot][:, 0:nich * 128].rearrange("p (c f) -> p c f", c=nich)
                        for tg in range(2):
                            def mmd(e, db=db, tg=tg, nich=nich):
                                ins = None
                                for ic in range(nich):
                                    ins = e.matmul(ps_d[:, tg * 512:(tg + 1) * 512], db[:, ic, :],
                                                   sfm_v[:, ic, tg * 512:(tg + 1) * 512],
                                                   start=(ic == 0), stop=(ic == nich - 1))
                                return ins
                            p.op("tensor", mmd, waits=[(s_bwd, nwd), (s_bgu, ngu), (s_bacc, dh_user[tg])],
                                 inc=s_bd)
                            if first_contrib:
                                p.op("scalar", lambda e, fc=fc, tg=tg: e.copy(
                                    out=acc_v[:, fc, tg * 512:(tg + 1) * 512],
                                    in_=ps_d[:, tg * 512:(tg + 1) * 512]),
                                    waits=[(s_bd, s_bd.n), (s_bcast, s_bcast.n)], inc=s_bacc)
                            else:
                                p.op("vector", lambda e, fc=fc, tg=tg: e.tensor_tensor(
                                    out=acc_v[:, fc, tg * 512:(tg + 1) * 512],
                                    in0=ps_d[:, tg * 512:(tg + 1) * 512],
                                    in1=acc_v[:, fc, tg * 512:(tg + 1) * 512], op=ALU.add),
                                    waits=[(s_bd, s_bd.n)], inc=s_bacc)
                            dh_user[tg] = s_bacc.n
                        wd_user[slot] = s_bd.n

                for q4 in range(4):
                    for k8 in range(8):
                        ic = q4 * 8 + k8
                        upgate(k8, sgn[ic * 128:(ic + 1) * 128, :].rearrange("p (c f) -> p c f", c=NC),
                               sun[ic * 128:(ic + 1) * 128, :].rearrange("p (c f) -> p c f", c=NC),
                               sgb, n_sgb, s_bsg)
                    downpass(8, lambda fc, q4=q4: sdn[fc * 128:(fc + 1) * 128,
                                                      q4 * 1024:(q4 + 1) * 1024].rearrange("p (c f) -> p c f", c=8),
                             first_contrib=(q4 == 0))
                for e_ in range(E):
                    for tg in range(2):
                        p.op("tensor", lambda e, e_=e_, tg=tg: e.matmul(
                            ps_m[:, 0:512], sel[:, e_ * 128:(e_ + 1) * 128],
                            combfm[:, tg * 512:(tg + 1) * 512], start=True, stop=True),
                            waits=[(s_brx, n_comb), (s_bcb, s_bcb.n), (s_bsg, n_sgb), (s_tbl, 1)], inc=s_brt)
                        p.op("scalar", lambda e, tg=tg: e.copy(out=cbs[:, tg * 512:(tg + 1) * 512],
                                                               in_=ps_m[:, 0:512]),
                             waits=[(s_brt, s_brt.n), (s_busc, s_busc.n)], inc=s_bcb)
                    ncb = s_bcb.n
                    for ic in range(NIE):
                        r = (e_ * NIE + ic) * 128
                        upgate(ic, wgn[e_ * II + ic * 128: e_ * II + (ic + 1) * 128, :].rearrange("p (c f) -> p c f", c=NC),
                               wun[e_ * II + ic * 128: e_ * II + (ic + 1) * 128, :].rearrange("p (c f) -> p c f", c=NC),
                               cbs, ncb, s_bcb)
                    downpass(NIE, lambda fc, e_=e_: wdn[e_ * Dm + fc * 128: e_ * Dm + (fc + 1) * 128,
                                                        :].rearrange("p (c f) -> p c f", c=8),
                             first_contrib=False)

                x1tr = x1f[:].rearrange("p (tt c f) -> p tt c f", tt=8, c=NC)
                for t in range(8):
                    ms = t % 4
                    tok0 = off + t * 128
                    p.op("scalar", lambda e, t=t: e.copy(
                        out=x1tr[:, t, :, :],
                        in_=acc_v[:, :, t * 128:(t + 1) * 128]),
                        waits=[(s_bacc, s_bacc.n), (s_bg, s_bg.n), (s_bu, s_bu.n),
                               (s_brt, s_brt.n), (s_bgu, s_bgu.n)], inc=s_bcast)
                    bcast_tile[t] = s_bcast.n
                    dma(p, "sync", moetm[ms][:].rearrange("p (c f) -> p c f", c=NC),
                        x1tr[:, t, :, :], s_btr,
                        waits=[(s_bcast, s_bcast.n), (s_bpre, mtm_user[ms])], transpose=True)
                    n_tr = s_btr.n
                    btr_tile[t] = n_tr
                    dma(p, "sync", x1resstg[:], x1tm_s[seq, tok0:tok0 + 128, :], s_bx1r,
                        waits=[(s_bpre, s_bpre.n)])
                    p.op("vector", lambda e, ms=ms: e.tensor_tensor(out=pre2[:], in0=moetm[ms][:],
                                                                    in1=x1resstg[:], op=ALU.add),
                         waits=[(s_btr, n_tr), (s_bx1r, s_bx1r.n), (s_bx2, s_bx2.n)], inc=s_bpre)
                    mtm_user[ms] = s_bpre.n
                    p.op("vector", lambda e: e.reduce_sum(out=b_ns[:], in_=pre2[:], axis=AX.X, negate=True),
                         waits=[(s_bpre, s_bpre.n), (s_bt1, s_bt1.n)], inc=s_bns)
                    p.op("scalar", lambda e: e.activation(out=x2st[:], in_=pre2[:],
                                                          func=AF.Square, accum_out=b_sq[:]),
                         waits=[(s_bpre, s_bpre.n), (s_bt3, s_bt3.n), (s_out, s_out.n)])
                    p.op("scalar", lambda e: e.copy(out=b_sq[:], in_=b_sq[:]), waits=[], inc=s_bsq)
                    p.op("vector", lambda e: e.tensor_scalar(out=b_nm[:], in0=b_ns[:], scalar1=1.0 / Dm,
                                                             scalar2=None, op0=ALU.mult),
                         waits=[(s_bns, s_bns.n), (s_bx2, s_bx2.n)], inc=s_bt1)
                    p.op("vector", lambda e: e.tensor_tensor(out=b_m2[:], in0=b_nm[:], in1=b_nm[:], op=ALU.mult),
                         waits=[(s_bt1, s_bt1.n), (s_bt3, s_bt3.n)], inc=s_bt2)
                    p.op("vector", lambda e: e.tensor_scalar(out=b_var[:], in0=b_sq[:], scalar1=1.0 / Dm,
                                                             scalar2=b_m2[:], op0=ALU.mult, op1=ALU.subtract),
                         waits=[(s_bsq, s_bsq.n), (s_bt2, s_bt2.n), (s_bst_, s_bst_.n)], inc=s_bt3)
                    p.op("scalar", lambda e: e.activation(out=b_std[:], in_=b_var[:], func=AF.Sqrt, bias=eps_col[:]),
                         waits=[(s_bt3, s_bt3.n), (s_brs, s_brs.n)], inc=s_bst_)
                    p.op("vector", lambda e: e.reciprocal(out=b_rs[:], in_=b_std[:]),
                         waits=[(s_bst_, s_bst_.n), (s_bx2, s_bx2.n)], inc=s_brs)
                    p.op("vector", lambda e: e.tensor_scalar(out=x2st[:], in0=pre2[:], scalar1=b_nm[:],
                                                             scalar2=b_rs[:], op0=ALU.add, op1=ALU.mult),
                         waits=[(s_brs, s_brs.n), (s_bsq, s_bsq.n), (s_out, s_out.n)], inc=s_bx2)
                    dma(p, "sync", out[seq, tok0:tok0 + 128, :], x2st[:], s_out,
                        waits=[(s_bx2, s_bx2.n)])

            for blk in range(NBLK):
                stage_b(blk)

            if x1dump is not None:
                dma(p, "sync", x1dump[:, :, :], x1tm_s[:, :, :], s_out, waits=[(s_x1w, n_x1w_all)])
            n_out = s_out.n
            p.op("sync", lambda e: e.wait_ge(s_out.h, n_out), waits=[])

            with nc.Block() as block:
                def emit(eng_name):
                    def run(eng):
                        for fn, ws, tgt in p.q[eng_name]:
                            for s, v in ws:
                                eng.wait_ge(s.h, v)
                            ins = fn(eng)
                            if tgt is not None:
                                s, k = tgt
                                ins.then_inc(s.h, k)
                    return run

                block.tensor(emit("tensor"))
                block.vector(emit("vector"))
                block.scalar(emit("scalar"))
                block.gpsimd(emit("gpsimd"))
                block.sync(emit("sync"))

    return nc


def _host_tables():
    inv = np.power(10000.0, -2.0 * np.arange(HD // 2, dtype=np.float64) / HD)
    t = np.arange(T, dtype=np.float64)
    ang = t[None, :] * inv[:, None]
    cos = np.cos(ang)
    sin = np.sin(ang)
    cosT = np.concatenate([cos, cos], axis=0).astype(np.float32)
    sinT = np.concatenate([-sin, sin], axis=0).astype(np.float32)
    return cosT, sinT, (cosT * SCALE).astype(np.float32), (sinT * SCALE).astype(np.float32)


_NC_CACHE = {}


def kernel(L0_attn_x, L0_attn_Wq, L0_attn_Wk, L0_attn_Wv, L0_attn_Wo,
           L0_ffn_w_router, L0_ffn_wg, L0_ffn_wu, L0_ffn_wd,
           L0_ffn_sw_gate, L0_ffn_sw_up, L0_ffn_sw_down, L0_ffn_gate_w):
    x = np.ascontiguousarray(np.asarray(L0_attn_x, dtype=np.float32))
    cosT, sinT, cosQ, sinQ = _host_tables()
    if "nc" not in _NC_CACHE:
        _NC_CACHE["nc"] = build_program()
    nc = _NC_CACHE["nc"]
    common = {
        "wq": np.ascontiguousarray(np.asarray(L0_attn_Wq, np.float32)),
        "wk": np.ascontiguousarray(np.asarray(L0_attn_Wk, np.float32)),
        "wv": np.ascontiguousarray(np.asarray(L0_attn_Wv, np.float32)),
        "wo": np.ascontiguousarray(np.asarray(L0_attn_Wo, np.float32)),
        "wrouter": np.ascontiguousarray(np.asarray(L0_ffn_w_router, np.float32)),
        "wg": np.ascontiguousarray(np.asarray(L0_ffn_wg, np.float32).reshape(E * II, Dm)),
        "wu": np.ascontiguousarray(np.asarray(L0_ffn_wu, np.float32).reshape(E * II, Dm)),
        "wd": np.ascontiguousarray(np.asarray(L0_ffn_wd, np.float32).reshape(E * Dm, II)),
        "swg": np.ascontiguousarray(np.asarray(L0_ffn_sw_gate, np.float32)),
        "swu": np.ascontiguousarray(np.asarray(L0_ffn_sw_up, np.float32)),
        "swd": np.ascontiguousarray(np.asarray(L0_ffn_sw_down, np.float32)),
        "gatew": np.ascontiguousarray(np.asarray(L0_ffn_gate_w, np.float32)),
        "cosT": cosT,
        "sinT": sinT,
        "cosQ": cosQ,
        "sinQ": sinQ,
    }
    in_maps = []
    for c in range(NCORES):
        m = dict(common)
        m["x"] = x[c * SEQ:(c + 1) * SEQ]
        in_maps.append(m)
    res = run_bass_kernel_spmd(nc, in_maps, core_ids=list(range(NCORES)))
    return np.concatenate([np.asarray(r["out"], np.float32) for r in res.results], axis=0)


# version 17
# speedup vs baseline: 1.0070x; 1.0070x over previous
import numpy as np
import ml_dtypes
import concourse.bass as bass
import concourse.mybir as mybir
from concourse.bass_utils import run_bass_kernel_spmd
from contextlib import ExitStack

F32 = mybir.dt.float32
BF16 = mybir.dt.bfloat16
AX = mybir.AxisListType
ALU = mybir.AluOpType
AF = mybir.ActivationFunctionType

B, T, Dm = 16, 2048, 2048
NH, NKV, HD = 16, 4, 128
E, II, IS = 8, 1024, 4096
NCORES = 8
SEQ = B // NCORES
NG = 4
GT = T // NG
NTT = T // 128
NC = Dm // 128
NIS = IS // 128
NIE = II // 128
BT = 1024
NBLK = (SEQ * T) // BT
SCALE = 1.0 / np.sqrt(HD)
NEG = -1.0e9

ENGINES = ("tensor", "vector", "scalar", "gpsimd", "sync")


class Sm:
    __slots__ = ("h", "n", "name")

    def __init__(self, h, name):
        self.h = h
        self.n = 0
        self.name = name


class Plan:
    def __init__(self):
        self.q = {e: [] for e in ENGINES}
        self.maxw = {e: {} for e in ENGINES}
        self.last_inc = {e: None for e in ENGINES}

    def op(self, eng, fn, waits=(), inc=None, k=1):
        ws = []
        seen = self.maxw[eng]
        for s, v in waits:
            if v <= 0:
                continue
            if seen.get(id(s), -1) >= v:
                continue
            seen[id(s)] = v
            ws.append((s, v))
        tgt = None
        if inc is not None:
            inc.n += k
            tgt = (inc, k)
            self.last_inc[eng] = (inc, inc.n)
        self.q[eng].append((fn, ws, tgt))

    def fence(self, engs=ENGINES):
        return [self.last_inc[e] for e in engs if self.last_inc[e] is not None]


def dma(plan, q, out, in_, sem, waits=(), transpose=False, slow=False):
    def fn(e, o=out, i=in_, t=transpose, sl=slow):
        return e.dma_start(out=o, in_=i, transpose=t, allow_slow_non_contiguous=sl)
    plan.op(q, fn, waits, sem, 16)


def build_program(debug_x1=False, a2_depth=2):
    nc = bass.Bass("TRN2", target_bir_lowering=False)
    p = Plan()

    xin = nc.dram_tensor("x", [SEQ, T, Dm], F32, kind="ExternalInput").ap()
    wq = nc.dram_tensor("wq", [Dm, Dm], F32, kind="ExternalInput").ap()
    wk = nc.dram_tensor("wk", [Dm, NKV * HD], F32, kind="ExternalInput").ap()
    wv = nc.dram_tensor("wv", [Dm, NKV * HD], F32, kind="ExternalInput").ap()
    wo = nc.dram_tensor("wo", [Dm, Dm], F32, kind="ExternalInput").ap()
    wrouter = nc.dram_tensor("wrouter", [E, Dm], F32, kind="ExternalInput").ap()
    wg_in = nc.dram_tensor("wg", [E * II, Dm], F32, kind="ExternalInput").ap()
    wu_in = nc.dram_tensor("wu", [E * II, Dm], F32, kind="ExternalInput").ap()
    wd_in = nc.dram_tensor("wd", [E * Dm, II], F32, kind="ExternalInput").ap()
    swg_in = nc.dram_tensor("swg", [IS, Dm], F32, kind="ExternalInput").ap()
    swu_in = nc.dram_tensor("swu", [IS, Dm], F32, kind="ExternalInput").ap()
    swd_in = nc.dram_tensor("swd", [Dm, IS], F32, kind="ExternalInput").ap()
    gatew = nc.dram_tensor("gatew", [1, Dm], F32, kind="ExternalInput").ap()
    cos_in = nc.dram_tensor("cosT", [HD, T], F32, kind="ExternalInput").ap()
    sin_in = nc.dram_tensor("sinT", [HD, T], F32, kind="ExternalInput").ap()
    cosq_in = nc.dram_tensor("cosQ", [HD, T], F32, kind="ExternalInput").ap()
    sinq_in = nc.dram_tensor("sinQ", [HD, T], F32, kind="ExternalInput").ap()
    out = nc.dram_tensor("out", [SEQ, T, Dm], F32, kind="ExternalOutput").ap()
    x1dump = nc.dram_tensor("x1dump", [SEQ, T, Dm], BF16, kind="ExternalOutput").ap() if debug_x1 else None

    wkb = nc.dram_tensor("wkb", [Dm, NKV * HD], BF16, kind="Internal").ap()
    wvb = nc.dram_tensor("wvb", [Dm, NKV * HD], BF16, kind="Internal").ap()
    wqb = nc.dram_tensor("wqb", [Dm, Dm], BF16, kind="Internal").ap()
    wob = nc.dram_tensor("wob", [Dm, Dm], BF16, kind="Internal").ap()
    sgn = nc.dram_tensor("sgn", [IS, Dm], BF16, kind="Internal").ap()
    sun = nc.dram_tensor("sun", [IS, Dm], BF16, kind="Internal").ap()
    wgn = nc.dram_tensor("wgn", [E * II, Dm], BF16, kind="Internal").ap()
    wun = nc.dram_tensor("wun", [E * II, Dm], BF16, kind="Internal").ap()
    sdn = nc.dram_tensor("sdn", [Dm, IS], BF16, kind="Internal").ap()
    wdn = nc.dram_tensor("wdn", [E * Dm, II], BF16, kind="Internal").ap()
    x1tm_s = nc.dram_tensor("x1tm_s", [SEQ, T, Dm], BF16, kind="Internal").ap()

    st = ExitStack()

    def sem(name):
        return Sm(st.enter_context(nc.semaphore(name)), name)

    with st:
        s_tbl = sem("tbl")

        ident = st.enter_context(nc.sbuf_tensor("ident", [128, 128], BF16))
        dmask = st.enter_context(nc.sbuf_tensor("dmask", [128, 128], BF16))
        sel = st.enter_context(nc.sbuf_tensor("sel", [E, E * 128], BF16))
        ones1 = st.enter_context(nc.sbuf_tensor("ones1", [1, 128], BF16))
        wrt = st.enter_context(nc.sbuf_tensor("wrt", [128, NC * E], BF16))
        gwt = st.enter_context(nc.sbuf_tensor("gwt", [128, NC], BF16))
        eps_col = st.enter_context(nc.sbuf_tensor("eps_col", [128, 1], F32))

        def gen_consts(gp):
            gp.memset(ident[:], 0.0)
            gp.affine_select(out=ident[:], in_=ident[:], compare_op=ALU.not_equal,
                             fill=1.0, base=0, pattern=[[-1, 128]], channel_multiplier=1)
            gp.memset(dmask[:], 0.0)
            gp.affine_select(out=dmask[:], in_=dmask[:], compare_op=ALU.is_ge,
                             fill=NEG, base=0, pattern=[[-1, 128]], channel_multiplier=1)
            gp.memset(sel[:], 0.0)
            for e_ in range(E):
                gp.affine_select(out=sel[:, e_ * 128:(e_ + 1) * 128], in_=sel[:, e_ * 128:(e_ + 1) * 128],
                                 compare_op=ALU.not_equal, fill=1.0, base=-e_,
                                 pattern=[[0, 128]], channel_multiplier=1)
            gp.memset(ones1[:], 1.0)
            return gp.memset(eps_col[:], 1e-5)

        p.op("gpsimd", gen_consts, inc=s_tbl)
        for c in range(NC):
            dma(p, "gpsimd", wrt[:, c * E:(c + 1) * E],
                wrouter[:, c * 128:(c + 1) * 128].rearrange("e p -> p e"), s_tbl, slow=True)
            dma(p, "gpsimd", gwt[:, c:c + 1],
                gatew[:, c * 128:(c + 1) * 128].rearrange("a p -> p a"), s_tbl, slow=True)

        s_pc = sem("pc")
        s_pck = sem("pck")
        s_pcv = sem("pcv")
        s_pcq = sem("pcq")
        s_pco = sem("pco")
        prep_jobs = []

        def add_cast(src2d, dst2d, nrows, rows_per, psem=None):
            for r0 in range(0, nrows, rows_per):
                prep_jobs.append((src2d[r0:r0 + rows_per, :], dst2d[r0:r0 + rows_per, :], psem or s_pc))

        add_cast(wk, wkb, Dm, 512, s_pck)
        N_WK = 4
        add_cast(wv, wvb, Dm, 512, s_pcv)
        N_WV = 4
        add_cast(wq, wqb, Dm, 256, s_pcq)
        N_WQ = 8
        add_cast(wo, wob, Dm, 256, s_pco)
        N_ATTN_PREP = 8
        add_cast(swg_in, sgn, IS, 256)
        add_cast(swu_in, sun, IS, 256)
        add_cast(wg_in, wgn, E * II, 256)
        add_cast(wu_in, wun, E * II, 256)
        add_cast(swd_in, sdn, Dm, 128)
        add_cast(wd_in, wdn, E * Dm, 512)
        NPREP = len(prep_jobs) - 24
        NPREP_ALL = len(prep_jobs)

        prep_state = {"i": 0}

        def emit_prep(njobs):
            for _ in range(njobs):
                i = prep_state["i"]
                if i >= NPREP_ALL:
                    return
                prep_state["i"] += 1
                src, dst, psem = prep_jobs[i]
                dma(p, "gpsimd", dst, src, psem)

        ast = ExitStack()
        with ast:
            cosT = ast.enter_context(nc.sbuf_tensor("cos_sb", [128, T], BF16))
            sinT = ast.enter_context(nc.sbuf_tensor("sin_sb", [128, T], BF16))
            cosQ = ast.enter_context(nc.sbuf_tensor("cosq_sb", [128, T], BF16))
            sinQ = ast.enter_context(nc.sbuf_tensor("sinq_sb", [128, T], BF16))
            k_rope = ast.enter_context(nc.sbuf_tensor("k_rope", [128, NKV * T], BF16))
            v_tm = ast.enter_context(nc.sbuf_tensor("v_tm", [128, NTT * 512], BF16))
            q_rope = ast.enter_context(nc.sbuf_tensor("q_rope", [128, NC * GT], BF16))
            xattn = ast.enter_context(nc.sbuf_tensor("xattn", [128, NC * GT], BF16))
            xofm = ast.enter_context(nc.sbuf_tensor("xofm", [128, NC * GT], BF16))
            xstg = ast.enter_context(nc.sbuf_tensor("xstg", [128, Dm], BF16))
            xtmp = ast.enter_context(nc.sbuf_tensor("xtmp", [128, Dm], BF16))
            wsl = [ast.enter_context(nc.sbuf_tensor(f"wsl{i}", [128, NC * 512], BF16)) for i in range(2)]
            rraw = [ast.enter_context(nc.sbuf_tensor(f"rraw{i}", [128, GT], F32)) for i in range(2)]
            rsh = [ast.enter_context(nc.sbuf_tensor(f"rsh{i}", [128, GT], F32)) for i in range(2)]
            prow = ast.enter_context(nc.sbuf_tensor("prow", [128, T], BF16))
            atb = ast.enter_context(nc.sbuf_tensor("atb", [128, T], BF16))
            ygrp = ast.enter_context(nc.sbuf_tensor("ygrp", [128, NH * 512], BF16))
            xres = ast.enter_context(nc.sbuf_tensor("xres", [128, Dm], F32))
            pre = ast.enter_context(nc.sbuf_tensor("pre", [128, Dm], F32))
            attn_tm = ast.enter_context(nc.sbuf_tensor("attn_tm", [128, Dm], BF16))
            x1st = ast.enter_context(nc.sbuf_tensor("x1st", [128, Dm], BF16))
            col_m = ast.enter_context(nc.sbuf_tensor("col_m", [128, 8], F32))
            col_b = ast.enter_context(nc.sbuf_tensor("col_b", [128, 2], F32))
            col_se = ast.enter_context(nc.sbuf_tensor("col_se", [128, 2], F32))
            col_rc = ast.enter_context(nc.sbuf_tensor("col_rc", [128, 2], F32))
            ln_ns = ast.enter_context(nc.sbuf_tensor("ln_ns", [128, 1], F32))
            ln_sq = ast.enter_context(nc.sbuf_tensor("ln_sq", [128, 1], F32))
            ln_nm = ast.enter_context(nc.sbuf_tensor("ln_nm", [128, 1], F32))
            ln_m2 = ast.enter_context(nc.sbuf_tensor("ln_m2", [128, 1], F32))
            ln_var = ast.enter_context(nc.sbuf_tensor("ln_var", [128, 1], F32))
            ln_std = ast.enter_context(nc.sbuf_tensor("ln_std", [128, 1], F32))
            ln_rs = ast.enter_context(nc.sbuf_tensor("ln_rs", [128, 1], F32))

            ps_row = ast.enter_context(nc.psum_tensor("ps_row", [128, T], F32))
            ps_y0 = ast.enter_context(nc.psum_tensor("ps_y0", [128, 512], F32))
            ps_y = [ps_y0, ps_y0]
            ps_a0 = ast.enter_context(nc.psum_tensor("ps_a0", [128, 512], F32))
            ps_a = [ps_a0, ps_a0]
            ps_at = ast.enter_context(nc.psum_tensor("ps_at", [128, T], BF16))

            s_tld = sem("tld")
            s_xld = sem("xld")
            s_xtr0 = sem("xtr0")
            s_xtr = sem("xtr")
            s_wld = sem("wld")
            s_pa = sem("pa")
            s_pev = sem("pev")
            s_rsh = sem("rsh")
            s_rt1 = sem("rt1")
            s_rt2 = sem("rt2")
            s_kq = sem("kq")
            s_v = sem("v")
            s_scr = sem("scr")
            s_msk = sem("msk")
            s_max = sem("max")
            s_bias = sem("bias")
            s_exp = sem("exp")
            s_rcp = sem("rcp")
            s_psc = sem("psc")
            s_att = sem("att")
            s_atb = sem("atb")
            s_av = sem("av")
            s_yev = sem("yev")
            s_oev = sem("oev")
            s_otr = sem("otr")
            s_xr = sem("xr")
            s_lpre = sem("lpre")
            s_lns = sem("lns")
            s_lsq = sem("lsq")
            s_lt1 = sem("lt1")
            s_lt2 = sem("lt2")
            s_lt3 = sem("lt3")
            s_lst = sem("lst")
            s_lrs = sem("lrs")
            s_x1 = sem("x1")
            s_x1w = sem("x1w")

            dma(p, "gpsimd", cosT[:], cos_in, s_tld)
            dma(p, "gpsimd", sinT[:], sin_in, s_tld)
            dma(p, "gpsimd", cosQ[:], cosq_in, s_tld)
            dma(p, "gpsimd", sinQ[:], sinq_in, s_tld)

            xattn_v = xattn[:].rearrange("p (c t) -> p c t", c=NC)
            xofm_t4 = xofm[:].rearrange("p (t c f) -> p t c f", t=NG, c=NC)

            wsl_user = [0, 0]
            slab_ctr = [0]
            rope_user = [0, 0]
            rope_ctr = [0]
            row_user = [0, 0]
            prow_user = [0, 0]
            psat_user = [0, 0]
            atb_user = [0, 0]
            se_user = [0, 0]
            y_user = [0, 0]
            a2_count = [0]
            yq_count = [0]
            xcopy_gate = [0]

            def rope_chunk(seq, g, psum, dst_col_ap, ctab, stab):
                sl = rope_ctr[0] % 2
                rope_ctr[0] += 1
                rw, rs = rraw[sl], rsh[sl]
                p.op("scalar", lambda e, ps=psum, rw=rw: e.copy(out=rw[:], in_=ps[:]),
                     waits=[(s_pa, s_pa.n), (s_kq, rope_user[sl])], inc=s_pev)
                n_ev = s_pev.n
                dma(p, "sync", rs[0:64, :], rw[64:128, :], s_rsh,
                    waits=[(s_pev, n_ev), (s_kq, rope_user[sl])])
                dma(p, "sync", rs[64:128, :], rw[0:64, :], s_rsh, waits=[(s_pev, n_ev)])
                n_sh = s_rsh.n
                p.op("vector", lambda e, sl_=(g * GT), rw=rw: e.tensor_tensor(
                    out=rw[:], in0=rw[:], in1=ctab[:, sl_:sl_ + GT], op=ALU.mult),
                    waits=[(s_pev, n_ev), (s_rsh, n_sh), (s_tld, 64)], inc=s_rt1)
                p.op("vector", lambda e, sl_=(g * GT), rs=rs: e.tensor_tensor(
                    out=rs[:], in0=rs[:], in1=stab[:, sl_:sl_ + GT], op=ALU.mult),
                    waits=[(s_rsh, n_sh), (s_tld, 64)], inc=s_rt2)
                p.op("vector", lambda e, d=dst_col_ap, rw=rw, rs=rs: e.tensor_tensor(
                    out=d, in0=rw[:], in1=rs[:], op=ALU.add),
                    waits=[(s_rt1, s_rt1.n), (s_rt2, s_rt2.n), (s_scr, s_scr.n)], inc=s_kq)
                rope_user[sl] = s_kq.n

            def load_slab(src_fn, ncols, psem, prep_n):
                slot = slab_ctr[0] % 2
                slab_ctr[0] += 1
                for c in range(NC):
                    dma(p, "gpsimd", wsl[slot][:, c * 512:c * 512 + ncols], src_fn(c), s_wld,
                        waits=[(s_pa, wsl_user[slot]), (psem, 16 * prep_n)])
                return slot, s_wld.n

            def a1_xpath(seq, g):
                for t in range(NG):
                    tok0 = g * GT + t * 128
                    dma(p, "gpsimd", xstg[:], xin[seq, tok0:tok0 + 128, :], s_xld,
                        waits=[(s_xtr0, s_xtr0.n)])
                    dma(p, "sync", xtmp[:].rearrange("p (c f) -> p c f", c=NC),
                        xstg[:].rearrange("p (c f) -> p c f", c=NC), s_xtr0,
                        waits=[(s_xld, s_xld.n), (s_xtr, s_xtr.n)], transpose=True)
                    p.op("scalar", lambda e, t=t: e.copy(out=xattn_v[:, :, t * 128:(t + 1) * 128],
                                                         in_=xtmp[:].rearrange("p (c f) -> p c f", c=NC)),
                         waits=[(s_xtr0, s_xtr0.n), (s_pa, xcopy_gate[0])], inc=s_xtr)

            def stage_a1(seq, g):
                n_xtr = s_xtr.n

                slot, nw = load_slab(lambda c: wkb[c * 128:(c + 1) * 128, :], 512, s_pck, N_WK)
                for o in range(NKV):
                    pb = ps_a[o % 2]
                    def mmk(e, o=o, pb=pb, slot=slot):
                        ins = None
                        for c in range(NC):
                            ins = e.matmul(pb[:], wsl[slot][:, c * 512 + o * 128: c * 512 + (o + 1) * 128],
                                           xattn_v[:, c, :], start=(c == 0), stop=(c == NC - 1))
                        return ins
                    p.op("tensor", mmk,
                         waits=[(s_wld, nw), (s_xtr, n_xtr), (s_pev, s_pev.n), (s_v, s_v.n),
                                (s_oev, s_oev.n), (s_tbl, 1)], inc=s_pa)
                    rope_chunk(seq, g, pb, k_rope[:, o * T + g * GT: o * T + (g + 1) * GT], cosT, sinT)
                wsl_user[slot] = s_pa.n
                slot, nw = load_slab(lambda c: wvb[c * 128:(c + 1) * 128, :], 512, s_pcv, N_WV)
                for t in range(NG):
                    pb = ps_a[t % 2]
                    def mmv(e, t=t, pb=pb, slot=slot):
                        ins = None
                        for c in range(NC):
                            ins = e.matmul(pb[:], xattn_v[:, c, t * 128:(t + 1) * 128],
                                           wsl[slot][:, c * 512:(c + 1) * 512],
                                           start=(c == 0), stop=(c == NC - 1))
                        return ins
                    p.op("tensor", mmv, waits=[(s_wld, nw), (s_pev, s_pev.n), (s_v, s_v.n),
                                               (s_oev, s_oev.n)], inc=s_pa)
                    tt = g * NG + t
                    p.op("scalar", lambda e, pb=pb, tt=tt: e.copy(
                        out=v_tm[:, tt * 512:(tt + 1) * 512], in_=pb[:]),
                        waits=[(s_pa, s_pa.n), (s_av, s_av.n)], inc=s_v)
                wsl_user[slot] = s_pa.n
                for qtr in range(4):
                    slot, nw = load_slab(lambda c, qtr=qtr: wqb[c * 128:(c + 1) * 128,
                                                               qtr * 512:(qtr + 1) * 512], 512, s_pcq, N_WQ)
                    for oo in range(4):
                        o = qtr * 4 + oo
                        pb = ps_a[o % 2]
                        def mmq(e, oo=oo, slot=slot, pb=pb):
                            ins = None
                            for c in range(NC):
                                ins = e.matmul(pb[:], wsl[slot][:, c * 512 + oo * 128: c * 512 + (oo + 1) * 128],
                                               xattn_v[:, c, :], start=(c == 0), stop=(c == NC - 1))
                            return ins
                        p.op("tensor", mmq, waits=[(s_wld, nw), (s_pev, s_pev.n), (s_v, s_v.n),
                                                   (s_oev, s_oev.n)], inc=s_pa)
                        rope_chunk(seq, g, pb, q_rope[:, o * GT:(o + 1) * GT], cosQ, sinQ)
                    wsl_user[slot] = s_pa.n
                xcopy_gate[0] = s_pa.n

            def stage_a2(seq, i, h):
                k = a2_count[0]
                a2_count[0] += 1
                g = h // 4
                w = (i // 4) * 512 + (i % 4 + 1) * 128
                nt = w // 128
                if i < 8:
                    halves = (k % 2,)
                    off = (k % 2) * 1024
                else:
                    halves = (0, 1)
                    off = 0
                sl = halves[0] if len(halves) == 1 else 0
                cs = sl if len(halves) == 1 else 0
                own = halves
                row_wait = max(row_user[h_] for h_ in own)
                prow_wait = max(prow_user[h_] for h_ in own)
                psat_wait = max(psat_user[h_] for h_ in own)
                atb_wait = max(atb_user[h_] for h_ in own)
                se_wait = max(se_user[h_] for h_ in own)
                pr_ap = prow[:, off:off + w]
                at_ap = atb[:, off:off + w]
                cm = col_m[:, cs * 4:cs * 4 + 4]
                cb = col_b[:, cs:cs + 1]
                cse = col_se[:, cs:cs + 1]
                crc = col_rc[:, cs:cs + 1]

                nblk = (w + 511) // 512
                for kb in range(nblk):
                    bw = min(512, w - kb * 512)
                    def mblk(e, i=i, h=h, g=g, off=off, kb=kb, bw=bw, last=(kb == nblk - 1)):
                        ins = e.matmul(ps_row[:, off + kb * 512: off + kb * 512 + bw],
                                       q_rope[:, h * GT + (i % 4) * 128: h * GT + (i % 4 + 1) * 128],
                                       k_rope[:, g * T + kb * 512: g * T + kb * 512 + bw],
                                       start=True, stop=not last)
                        if last:
                            ins = e.matmul(ps_row[:, off + w - 128: off + w], ident[:], dmask[:],
                                           start=False, stop=True)
                        return ins
                    p.op("tensor", mblk,
                         waits=([(s_kq, (seq * NG + i // 4 + 1) * 20), (s_exp, row_wait),
                                 (s_tbl, 1)] if kb == 0 else []),
                         inc=s_scr)
                    n_scr = s_scr.n
                    p.op("vector", lambda e, off=off, kb=kb, bw=bw, cm=cm: e.reduce_max(
                        out=cm[:, kb:kb + 1], in_=ps_row[:, off + kb * 512: off + kb * 512 + bw], axis=AX.X),
                        waits=[(s_scr, n_scr), (s_bias, s_bias.n)], inc=s_max)
                p.op("vector", lambda e, cm=cm, cb=cb, nblk=nblk: e.reduce_max(
                    out=cb, in_=cm[:, 0:nblk], axis=AX.X, negate=True),
                    waits=[(s_max, s_max.n), (s_exp, s_exp.n)], inc=s_bias)
                p.op("scalar", lambda e, w=w, off=off, pr=pr_ap, cb=cb, cse=cse: e.activation(
                    out=pr, in_=ps_row[:, off:off + w], func=AF.Exp,
                    bias=cb, scale=1.0, accum_out=cse),
                    waits=[(s_bias, s_bias.n), (s_att, prow_wait), (s_rcp, se_wait)])
                p.op("scalar", lambda e, cse=cse: e.copy(out=cse, in_=cse), waits=[], inc=s_exp)
                for h_ in own:
                    row_user[h_] = s_exp.n
                p.op("vector", lambda e, cse=cse, crc=crc: e.reciprocal(out=crc, in_=cse),
                     waits=[(s_exp, s_exp.n)], inc=s_rcp)
                for h_ in own:
                    se_user[h_] = s_rcp.n
                p.op("vector", lambda e, pr=pr_ap, crc=crc: e.tensor_scalar(
                    out=pr, in0=pr, scalar1=crc, scalar2=None, op0=ALU.mult),
                    waits=[(s_rcp, s_rcp.n), (s_att, prow_wait)], inc=s_psc)
                n_psc = s_psc.n

                def emit_t(i=i, nt=nt, off=off, own=own, w=w, n_psc=n_psc,
                           psat_wait=psat_wait, atb_wait=atb_wait):
                    def mmt(e):
                        ins = None
                        for j in range(nt):
                            ins = e.transpose(ps_at[:, off + j * 128: off + (j + 1) * 128],
                                              prow[:, off + j * 128: off + (j + 1) * 128], ident[:])
                        return ins
                    p.op("tensor", mmt,
                         waits=[(s_psc, n_psc), (s_atb, psat_wait)], inc=s_att)
                    for h_ in own:
                        prow_user[h_] = s_att.n
                    p.op("vector", lambda e, off=off, w=w: e.tensor_copy(
                        out=atb[:, off:off + w], in_=ps_at[:, off:off + w]),
                        waits=[(s_att, s_att.n), (s_av, atb_wait)], inc=s_atb)
                    for h_ in own:
                        psat_user[h_] = s_atb.n
                    return s_atb.n

                def emit_av(seq=seq, i=i, h=h, g=g, nt=nt, off=off, own=own):
                    qd = yq_count[0]
                    pyb = ps_y[qd % 2]
                    def mmav(e, g=g, h=h, nt=nt, off=off, pyb=pyb):
                        ins = None
                        for j in range(nt):
                            ins = e.matmul(pyb[:, (h % 4) * 128:(h % 4 + 1) * 128],
                                           v_tm[:, j * 512 + g * 128: j * 512 + (g + 1) * 128],
                                           atb[:, off + j * 128: off + (j + 1) * 128],
                                           start=(j == 0), stop=(j == nt - 1))
                        return ins
                    p.op("tensor", mmav,
                         waits=[(s_atb, s_atb.n), (s_v, (seq * NG + i // 4 + 1) * NG),
                                (s_yev, y_user[qd % 2])],
                         inc=s_av)
                    for h_ in own:
                        atb_user[h_] = s_av.n
                    if h % 4 == 3:
                        hq = h - 3
                        def yev(e, i=i, hq=hq, pyb=pyb):
                            og = ygrp[:].rearrange("p (hh t) -> p hh t", hh=NH)
                            return e.copy(out=og[:, hq:hq + 4, (i % 4) * 128:(i % 4 + 1) * 128],
                                          in_=pyb[:].rearrange("p (hh t) -> p hh t", hh=4))
                        p.op("scalar", yev, waits=[(s_av, s_av.n), (s_pa, s_pa.n)], inc=s_yev)
                        y_user[0] = y_user[1] = s_yev.n
                        yq_count[0] += 1
                return emit_t, emit_av

            def stage_a3(seq, g):
                for qtr in range(4):
                    slot, nw = load_slab(lambda c, qtr=qtr: wob[c * 128:(c + 1) * 128,
                                                               qtr * 512:(qtr + 1) * 512], 512, s_pco, N_ATTN_PREP)
                    for ff in range(4):
                        f = qtr * 4 + ff
                        pb = ps_a[f % 2]
                        def mmo(e, ff=ff, slot=slot, pb=pb):
                            ins = None
                            for c in range(NC):
                                ins = e.matmul(pb[:], wsl[slot][:, c * 512 + ff * 128: c * 512 + (ff + 1) * 128],
                                               ygrp[:, c * 512:(c + 1) * 512],
                                               start=(c == 0), stop=(c == NC - 1))
                            return ins
                        p.op("tensor", mmo,
                             waits=[(s_wld, nw), (s_yev, yq_count[0]), (s_pev, s_pev.n),
                                    (s_oev, s_oev.n)],
                             inc=s_pa)
                        p.op("scalar", lambda e, pb=pb, f=f: e.copy(
                            out=xofm_t4[:, :, f, :],
                            in_=pb[:].rearrange("p (t f) -> p t f", t=NG)),
                            waits=[(s_pa, s_pa.n), (s_otr, s_otr.n)], inc=s_oev)
                    wsl_user[slot] = s_pa.n
                for t in range(NG):
                    tok0 = g * GT + t * 128
                    dma(p, "sync", attn_tm[:].rearrange("p (c f) -> p c f", c=NC),
                        xofm_t4[:, t, :, :], s_otr,
                        waits=[(s_oev, s_oev.n), (s_lpre, s_lpre.n)], transpose=True)
                    dma(p, "sync", xres[:], xin[seq, tok0:tok0 + 128, :], s_xr,
                        waits=[(s_lsq, s_lsq.n)])
                    p.op("vector", lambda e: e.tensor_tensor(out=pre[:], in0=attn_tm[:], in1=xres[:], op=ALU.add),
                         waits=[(s_otr, s_otr.n), (s_xr, s_xr.n), (s_x1, s_x1.n)], inc=s_lpre)
                    p.op("vector", lambda e: e.reduce_sum(out=ln_ns[:], in_=pre[:], axis=AX.X, negate=True),
                         waits=[(s_lpre, s_lpre.n), (s_lt1, s_lt1.n)], inc=s_lns)
                    p.op("scalar", lambda e: e.activation(out=xres[:], in_=pre[:], func=AF.Square,
                                                          accum_out=ln_sq[:]),
                         waits=[(s_lpre, s_lpre.n), (s_lt3, s_lt3.n)])
                    p.op("scalar", lambda e: e.copy(out=ln_sq[:], in_=ln_sq[:]), waits=[], inc=s_lsq)
                    p.op("vector", lambda e: e.tensor_scalar(out=ln_nm[:], in0=ln_ns[:], scalar1=1.0 / Dm,
                                                             scalar2=None, op0=ALU.mult),
                         waits=[(s_lns, s_lns.n), (s_x1, s_x1.n)], inc=s_lt1)
                    p.op("vector", lambda e: e.tensor_tensor(out=ln_m2[:], in0=ln_nm[:], in1=ln_nm[:], op=ALU.mult),
                         waits=[(s_lt1, s_lt1.n), (s_lt3, s_lt3.n)], inc=s_lt2)
                    p.op("vector", lambda e: e.tensor_scalar(out=ln_var[:], in0=ln_sq[:], scalar1=1.0 / Dm,
                                                             scalar2=ln_m2[:], op0=ALU.mult, op1=ALU.subtract),
                         waits=[(s_lsq, s_lsq.n), (s_lt2, s_lt2.n), (s_lst, s_lst.n)], inc=s_lt3)
                    p.op("scalar", lambda e: e.activation(out=ln_std[:], in_=ln_var[:], func=AF.Sqrt, bias=eps_col[:]),
                         waits=[(s_lt3, s_lt3.n), (s_lrs, s_lrs.n)], inc=s_lst)
                    p.op("vector", lambda e: e.reciprocal(out=ln_rs[:], in_=ln_std[:]),
                         waits=[(s_lst, s_lst.n), (s_x1, s_x1.n)], inc=s_lrs)
                    p.op("vector", lambda e: e.tensor_scalar(out=x1st[:], in0=pre[:], scalar1=ln_nm[:],
                                                             scalar2=ln_rs[:], op0=ALU.add, op1=ALU.mult),
                         waits=[(s_lrs, s_lrs.n), (s_x1w, s_x1w.n)], inc=s_x1)
                    dma(p, "sync", x1tm_s[seq, tok0:tok0 + 128, :], x1st[:], s_x1w,
                        waits=[(s_x1, s_x1.n)])

            emit_prep(24)
            a1_xpath(0, 0)
            for seq in range(SEQ):
                for g in range(NG):
                    stage_a1(seq, g)
                    emit_prep(3)
                    pend_t = []
                    pend_av = []

                    def flushq():
                        if pend_av:
                            pend_av.pop(0)()
                        while pend_t:
                            pend_t.pop(0)()
                            if pend_av:
                                pend_av.pop(0)()
                        while pend_av:
                            pend_av.pop(0)()

                    for it in range(NG):
                        i = g * NG + it
                        for h in range(NH):
                            if i >= 8:
                                flushq()
                            ft, fav = stage_a2(seq, i, h)
                            if i < 8:
                                if len(pend_av) >= 2:
                                    pend_av.pop(0)()
                                if pend_t:
                                    pend_t.pop(0)()
                                pend_t.append(ft)
                                pend_av.append(fav)
                            else:
                                ft()
                                fav()
                            emit_prep(1)
                        if it == 1:
                            if g + 1 < NG:
                                a1_xpath(seq, g + 1)
                            elif seq + 1 < SEQ:
                                a1_xpath(seq + 1, 0)
                    flushq()
                    stage_a3(seq, g)
                    emit_prep(6)
            emit_prep(10 ** 9)
            n_x1w_all = s_x1w.n
            a_fence = p.fence()

        bst = ExitStack()
        with bst:
            x1f = bst.enter_context(nc.sbuf_tensor("x1f", [128, NC * BT], BF16))
            x1ldstg = bst.enter_context(nc.sbuf_tensor("x1ldstg", [128, Dm], BF16))
            x1resstg = bst.enter_context(nc.sbuf_tensor("x1resstg", [128, Dm], BF16))
            xtmpb = bst.enter_context(nc.sbuf_tensor("xtmpb", [128, Dm], BF16))
            acc = bst.enter_context(nc.sbuf_tensor("acc", [128, NC * BT], F32))
            sfm = bst.enter_context(nc.sbuf_tensor("sfm", [128, 8 * BT], BF16))
            gbuf = [bst.enter_context(nc.sbuf_tensor(f"gbuf{i}", [128, Dm], BF16)) for i in range(2)]
            ubuf = [bst.enter_context(nc.sbuf_tensor(f"ubuf{i}", [128, Dm], BF16)) for i in range(2)]
            dbuf = [bst.enter_context(nc.sbuf_tensor(f"dbuf{i}", [128, 1024], BF16)) for i in range(2)]
            cbs = bst.enter_context(nc.sbuf_tensor("cbs", [128, BT], F32))
            sgb = bst.enter_context(nc.sbuf_tensor("sgb", [128, BT], F32))
            sgrow = bst.enter_context(nc.sbuf_tensor("sgrow", [1, BT], F32))
            sgrow_b = bst.enter_context(nc.sbuf_tensor("sgrow_b", [1, BT], BF16))
            gs = bst.enter_context(nc.sbuf_tensor("gs", [128, BT], BF16))
            usc = bst.enter_context(nc.sbuf_tensor("usc", [128, BT], BF16))
            rp = bst.enter_context(nc.sbuf_tensor("rp", [128, E], F32))
            rm1 = bst.enter_context(nc.sbuf_tensor("rm1", [128, 1], F32))
            rmask = bst.enter_context(nc.sbuf_tensor("rmask", [128, E], F32))
            rp2 = bst.enter_context(nc.sbuf_tensor("rp2", [128, E], F32))
            rm2 = bst.enter_context(nc.sbuf_tensor("rm2", [128, 1], F32))
            rge = bst.enter_context(nc.sbuf_tensor("rge", [128, E], F32))
            rcr = bst.enter_context(nc.sbuf_tensor("rcr", [128, E], F32))
            rden = bst.enter_context(nc.sbuf_tensor("rden", [128, 1], F32))
            rrc = bst.enter_context(nc.sbuf_tensor("rrc", [128, 1], F32))
            rcomb = bst.enter_context(nc.sbuf_tensor("rcomb", [128, E], BF16))
            combfm = bst.enter_context(nc.sbuf_tensor("combfm", [E, BT], BF16))
            moetm = [bst.enter_context(nc.sbuf_tensor(f"moetm{i}", [128, Dm], BF16)) for i in range(4)]
            pre2 = bst.enter_context(nc.sbuf_tensor("pre2", [128, Dm], F32))
            x2st = bst.enter_context(nc.sbuf_tensor("x2st", [128, Dm], F32))
            b_ns = bst.enter_context(nc.sbuf_tensor("b_ns", [128, 1], F32))
            b_sq = bst.enter_context(nc.sbuf_tensor("b_sq", [128, 1], F32))
            b_nm = bst.enter_context(nc.sbuf_tensor("b_nm", [128, 1], F32))
            b_m2 = bst.enter_context(nc.sbuf_tensor("b_m2", [128, 1], F32))
            b_var = bst.enter_context(nc.sbuf_tensor("b_var", [128, 1], F32))
            b_std = bst.enter_context(nc.sbuf_tensor("b_std", [128, 1], F32))
            b_rs = bst.enter_context(nc.sbuf_tensor("b_rs", [128, 1], F32))

            ps_g = bst.enter_context(nc.psum_tensor("ps_g", [128, 1024], F32))
            ps_u = bst.enter_context(nc.psum_tensor("ps_u", [128, 1024], F32))
            ps_d = bst.enter_context(nc.psum_tensor("ps_d", [128, 1024], F32))
            ps_m = bst.enter_context(nc.psum_tensor("ps_m", [128, 512], F32))
            ps_rt = bst.enter_context(nc.psum_tensor("ps_rt", [E, 128], BF16))

            s_bx1 = sem("bx1")
            s_bx1t = sem("bx1t")
            s_bx1c = sem("bx1c")
            s_bwg = sem("bwg")
            s_bwd = sem("bwd")
            s_bg = sem("bg")
            s_bu = sem("bu")
            s_bsil = sem("bsil")
            s_busc = sem("busc")
            s_bgu = sem("bgu")
            s_bd = sem("bd")
            s_bacc = sem("bacc")
            s_brt = sem("brt")
            s_brx = sem("brx")
            s_bcb = sem("bcb")
            s_bsg = sem("bsg")
            s_bcast = sem("bcast")
            s_btr = sem("btr")
            s_bx1r = sem("bx1r")
            s_bpre = sem("bpre")
            s_bns = sem("bns")
            s_bsq = sem("bsq")
            s_bt1 = sem("bt1")
            s_bt2 = sem("bt2")
            s_bt3 = sem("bt3")
            s_bst_ = sem("bst")
            s_brs = sem("brs")
            s_bx2 = sem("bx2")
            s_out = sem("sout")

            x1f_v = x1f[:].rearrange("p (c t) -> p c t", c=NC)
            acc_v = acc[:].rearrange("p (c t) -> p c t", c=NC)
            sfm_v = sfm[:].rearrange("p (c t) -> p c t", c=8)

            first_b = [True]
            wgu_user = [0, 0]
            wd_user = [0, 0]
            gh_user = [0, 0]
            uh_user = [0, 0]
            dh_user = [0, 0]
            ug_ctr = [0]
            dn_ctr = [0]
            mtm_user = [0, 0, 0, 0]
            btr_tile = [0] * 8
            bcast_tile = [0] * 8

            def stage_b(blk):
                seq = blk // 2
                off = (blk % 2) * BT
                fence = a_fence if first_b[0] else []
                base_x1c = s_bx1c.n
                for t in range(8):
                    tok0 = off + t * 128
                    dma(p, "sync", x1ldstg[:], x1tm_s[seq, tok0:tok0 + 128, :], s_bx1,
                        waits=[(s_x1w, n_x1w_all), (s_bx1t, s_bx1t.n)] + fence)
                    fence = []
                    dma(p, "sync", xtmpb[:].rearrange("p (c f) -> p c f", c=NC),
                        x1ldstg[:].rearrange("p (c f) -> p c f", c=NC), s_bx1t,
                        waits=[(s_bx1, s_bx1.n), (s_bx1c, s_bx1c.n)], transpose=True)
                    p.op("scalar", lambda e, t=t: e.copy(out=x1f_v[:, :, t * 128:(t + 1) * 128],
                                                         in_=xtmpb[:].rearrange("p (c f) -> p c f", c=NC)),
                         waits=[(s_bx1t, s_bx1t.n), (s_bg, s_bg.n), (s_bu, s_bu.n), (s_brt, s_brt.n),
                                (s_btr, btr_tile[t]), (s_bcast, bcast_tile[t])],
                         inc=s_bx1c)
                first_b[0] = False
                n_x1t = s_bx1c.n

                for t in range(8):
                    def mmr(e, t=t):
                        ins = None
                        for c in range(NC):
                            ins = e.matmul(ps_m[:, 0:E], x1f_v[:, c, t * 128:(t + 1) * 128],
                                           wrt[:, c * E:(c + 1) * E], start=(c == 0), stop=(c == NC - 1))
                        return ins
                    p.op("tensor", mmr, waits=[(s_bx1c, base_x1c + t + 1), (s_brx, s_brx.n),
                                               (s_tbl, s_tbl.n)], inc=s_brt)
                    p.op("vector", lambda e: e.reduce_max(out=rm1[:], in_=ps_m[:, 0:E], axis=AX.X, negate=True),
                         waits=[(s_brt, s_brt.n)], inc=s_brx)
                    p.op("scalar", lambda e: e.activation(out=rp[:], in_=ps_m[:, 0:E], func=AF.Exp, bias=rm1[:]),
                         waits=[(s_brx, s_brx.n)], inc=s_brx)
                    p.op("vector", lambda e: e.reduce_max(out=rm1[:], in_=rp[:], axis=AX.X),
                         waits=[(s_brx, s_brx.n)], inc=s_brx)
                    p.op("vector", lambda e: e.tensor_scalar(out=rmask[:], in0=rp[:], scalar1=rm1[:],
                                                             scalar2=None, op0=ALU.is_ge),
                         waits=[(s_brx, s_brx.n)], inc=s_brx)
                    p.op("vector", lambda e: e.scalar_tensor_tensor(out=rp2[:], in0=rmask[:], scalar=NEG,
                                                                    in1=rp[:], op0=ALU.mult, op1=ALU.add),
                         waits=[(s_brx, s_brx.n)], inc=s_brx)
                    p.op("vector", lambda e: e.reduce_max(out=rm2[:], in_=rp2[:], axis=AX.X),
                         waits=[(s_brx, s_brx.n)], inc=s_brx)
                    p.op("vector", lambda e: e.tensor_scalar(out=rge[:], in0=rp[:], scalar1=rm2[:],
                                                             scalar2=None, op0=ALU.is_ge),
                         waits=[(s_brx, s_brx.n)], inc=s_brx)
                    p.op("vector", lambda e: e.tensor_tensor(out=rcr[:], in0=rp[:], in1=rge[:], op=ALU.mult),
                         waits=[(s_brx, s_brx.n)], inc=s_brx)
                    p.op("vector", lambda e: e.reduce_sum(out=rden[:], in_=rcr[:], axis=AX.X),
                         waits=[(s_brx, s_brx.n)], inc=s_brx)
                    p.op("vector", lambda e: e.reciprocal(out=rrc[:], in_=rden[:]),
                         waits=[(s_brx, s_brx.n)], inc=s_brx)
                    p.op("vector", lambda e: e.tensor_scalar(out=rcomb[:], in0=rcr[:], scalar1=rrc[:],
                                                             scalar2=None, op0=ALU.mult),
                         waits=[(s_brx, s_brx.n)], inc=s_brx)
                    p.op("tensor", lambda e: e.transpose(ps_rt[:], rcomb[:], ident[:]),
                         waits=[(s_brx, s_brx.n), (s_bcb, s_bcb.n)], inc=s_brt)
                    p.op("vector", lambda e, t=t: e.tensor_copy(out=combfm[:, t * 128:(t + 1) * 128], in_=ps_rt[:]),
                         waits=[(s_brt, s_brt.n)], inc=s_brx)
                n_comb = s_brx.n

                for tg in range(2):
                    def mmsg2(e, tg=tg):
                        ins = None
                        for c in range(NC):
                            ins = e.matmul(ps_m[0:1, 0:512], gwt[:, c:c + 1],
                                           x1f_v[:, c, tg * 512:(tg + 1) * 512],
                                           start=(c == 0), stop=(c == NC - 1))
                        return ins
                    p.op("tensor", mmsg2, waits=[(s_bx1c, n_x1t), (s_bsg, s_bsg.n), (s_brx, s_brx.n)], inc=s_brt)
                    p.op("scalar", lambda e, tg=tg: e.activation(out=sgrow[:, tg * 512:(tg + 1) * 512],
                                                                 in_=ps_m[0:1, 0:512], func=AF.Sigmoid),
                         waits=[(s_brt, s_brt.n)], inc=s_bsg)
                p.op("vector", lambda e: e.tensor_copy(out=sgrow_b[:], in_=sgrow[:]),
                     waits=[(s_bsg, s_bsg.n)], inc=s_bsg)
                for tg in range(2):
                    p.op("tensor", lambda e, tg=tg: e.matmul(ps_m[:, 0:512], ones1[:],
                                                             sgrow_b[:, tg * 512:(tg + 1) * 512],
                                                             start=True, stop=True),
                         waits=[(s_bsg, s_bsg.n), (s_brx, n_comb)], inc=s_brt)
                    p.op("scalar", lambda e, tg=tg: e.copy(out=sgb[:, tg * 512:(tg + 1) * 512], in_=ps_m[:, 0:512]),
                         waits=[(s_brt, s_brt.n), (s_busc, s_busc.n)], inc=s_bsg)
                n_sgb = s_bsg.n

                def upgate(i_local, gsrc_rows, usrc_rows, scale_sb, n_scale_ready, scale_sem):
                    slot = ug_ctr[0] % 2
                    ug_ctr[0] += 1
                    dma(p, "sync", gbuf[slot][:].rearrange("p (c f) -> p c f", c=NC), gsrc_rows, s_bwg,
                        waits=[(s_bg, wgu_user[slot]), (s_pc, 16 * NPREP)], transpose=True)
                    dma(p, "sync", ubuf[slot][:].rearrange("p (c f) -> p c f", c=NC), usrc_rows, s_bwg,
                        waits=[(s_bu, wgu_user[slot])], transpose=True)
                    nw = s_bwg.n
                    gb = gbuf[slot][:].rearrange("p (c f) -> p c f", c=NC)
                    ub = ubuf[slot][:].rearrange("p (c f) -> p c f", c=NC)
                    for tg in range(2):
                        def mmg(e, gb=gb, tg=tg):
                            ins = None
                            for c in range(NC):
                                ins = e.matmul(ps_g[:, tg * 512:(tg + 1) * 512], gb[:, c, :],
                                               x1f_v[:, c, tg * 512:(tg + 1) * 512],
                                               start=(c == 0), stop=(c == NC - 1))
                            return ins
                        p.op("tensor", mmg, waits=[(s_bwg, nw), (s_bx1c, n_x1t), (s_bsil, gh_user[tg])],
                             inc=s_bg)
                        p.op("scalar", lambda e, tg=tg: e.activation(
                            out=gs[:, tg * 512:(tg + 1) * 512], in_=ps_g[:, tg * 512:(tg + 1) * 512],
                            func=AF.Silu),
                            waits=[(s_bg, s_bg.n), (s_bgu, s_bgu.n)], inc=s_bsil)
                        gh_user[tg] = s_bsil.n
                    n_bg_here = s_bg.n
                    for tg in range(2):
                        def mmu(e, ub=ub, tg=tg):
                            ins = None
                            for c in range(NC):
                                ins = e.matmul(ps_u[:, tg * 512:(tg + 1) * 512], ub[:, c, :],
                                               x1f_v[:, c, tg * 512:(tg + 1) * 512],
                                               start=(c == 0), stop=(c == NC - 1))
                            return ins
                        p.op("tensor", mmu, waits=[(s_bwg, nw), (s_busc, uh_user[tg])], inc=s_bu)
                        p.op("vector", lambda e, tg=tg, sc=scale_sb: e.tensor_tensor(
                            out=usc[:, tg * 512:(tg + 1) * 512], in0=ps_u[:, tg * 512:(tg + 1) * 512],
                            in1=sc[:, tg * 512:(tg + 1) * 512], op=ALU.mult),
                            waits=[(s_bu, s_bu.n), (scale_sem, n_scale_ready), (s_bgu, s_bgu.n)], inc=s_busc)
                        uh_user[tg] = s_busc.n
                        p.op("gpsimd", lambda e, tg=tg, il=i_local: e.tensor_tensor(
                            out=sfm_v[:, il, tg * 512:(tg + 1) * 512],
                            in0=gs[:, tg * 512:(tg + 1) * 512], in1=usc[:, tg * 512:(tg + 1) * 512],
                            op=ALU.mult),
                            waits=[(s_bsil, s_bsil.n), (s_busc, s_busc.n), (s_bd, s_bd.n)], inc=s_bgu)
                    wgu_user[slot] = n_bg_here

                def downpass(nich, dsrc_rows_fn, first_contrib):
                    ngu = s_bgu.n
                    for fc in range(NC):
                        slot = dn_ctr[0] % 2
                        dn_ctr[0] += 1
                        dma(p, "sync", dbuf[slot][:, 0:nich * 128].rearrange("p (c f) -> p c f", c=nich),
                            dsrc_rows_fn(fc), s_bwd,
                            waits=[(s_bd, wd_user[slot]), (s_pc, 16 * NPREP)], transpose=True)
                        nwd = s_bwd.n
                        db = dbuf[slot][:, 0:nich * 128].rearrange("p (c f) -> p c f", c=nich)
                        for tg in range(2):
                            def mmd(e, db=db, tg=tg, nich=nich):
                                ins = None
                                for ic in range(nich):
                                    ins = e.matmul(ps_d[:, tg * 512:(tg + 1) * 512], db[:, ic, :],
                                                   sfm_v[:, ic, tg * 512:(tg + 1) * 512],
                                                   start=(ic == 0), stop=(ic == nich - 1))
                                return ins
                            p.op("tensor", mmd, waits=[(s_bwd, nwd), (s_bgu, ngu), (s_bacc, dh_user[tg])],
                                 inc=s_bd)
                            if first_contrib:
                                p.op("scalar", lambda e, fc=fc, tg=tg: e.copy(
                                    out=acc_v[:, fc, tg * 512:(tg + 1) * 512],
                                    in_=ps_d[:, tg * 512:(tg + 1) * 512]),
                                    waits=[(s_bd, s_bd.n), (s_bcast, s_bcast.n)], inc=s_bacc)
                            else:
                                p.op("vector", lambda e, fc=fc, tg=tg: e.tensor_tensor(
                                    out=acc_v[:, fc, tg * 512:(tg + 1) * 512],
                                    in0=ps_d[:, tg * 512:(tg + 1) * 512],
                                    in1=acc_v[:, fc, tg * 512:(tg + 1) * 512], op=ALU.add),
                                    waits=[(s_bd, s_bd.n)], inc=s_bacc)
                            dh_user[tg] = s_bacc.n
                        wd_user[slot] = s_bd.n

                for q4 in range(4):
                    for k8 in range(8):
                        ic = q4 * 8 + k8
                        upgate(k8, sgn[ic * 128:(ic + 1) * 128, :].rearrange("p (c f) -> p c f", c=NC),
                               sun[ic * 128:(ic + 1) * 128, :].rearrange("p (c f) -> p c f", c=NC),
                               sgb, n_sgb, s_bsg)
                    downpass(8, lambda fc, q4=q4: sdn[fc * 128:(fc + 1) * 128,
                                                      q4 * 1024:(q4 + 1) * 1024].rearrange("p (c f) -> p c f", c=8),
                             first_contrib=(q4 == 0))
                for e_ in range(E):
                    for tg in range(2):
                        p.op("tensor", lambda e, e_=e_, tg=tg: e.matmul(
                            ps_m[:, 0:512], sel[:, e_ * 128:(e_ + 1) * 128],
                            combfm[:, tg * 512:(tg + 1) * 512], start=True, stop=True),
                            waits=[(s_brx, n_comb), (s_bcb, s_bcb.n), (s_bsg, n_sgb), (s_tbl, 1)], inc=s_brt)
                        p.op("scalar", lambda e, tg=tg: e.copy(out=cbs[:, tg * 512:(tg + 1) * 512],
                                                               in_=ps_m[:, 0:512]),
                             waits=[(s_brt, s_brt.n), (s_busc, s_busc.n)], inc=s_bcb)
                    ncb = s_bcb.n
                    for ic in range(NIE):
                        r = (e_ * NIE + ic) * 128
                        upgate(ic, wgn[e_ * II + ic * 128: e_ * II + (ic + 1) * 128, :].rearrange("p (c f) -> p c f", c=NC),
                               wun[e_ * II + ic * 128: e_ * II + (ic + 1) * 128, :].rearrange("p (c f) -> p c f", c=NC),
                               cbs, ncb, s_bcb)
                    downpass(NIE, lambda fc, e_=e_: wdn[e_ * Dm + fc * 128: e_ * Dm + (fc + 1) * 128,
                                                        :].rearrange("p (c f) -> p c f", c=8),
                             first_contrib=False)

                x1tr = x1f[:].rearrange("p (tt c f) -> p tt c f", tt=8, c=NC)
                for t in range(8):
                    ms = t % 4
                    tok0 = off + t * 128
                    p.op("scalar", lambda e, t=t: e.copy(
                        out=x1tr[:, t, :, :],
                        in_=acc_v[:, :, t * 128:(t + 1) * 128]),
                        waits=[(s_bacc, s_bacc.n), (s_bg, s_bg.n), (s_bu, s_bu.n),
                               (s_brt, s_brt.n), (s_bgu, s_bgu.n)], inc=s_bcast)
                    bcast_tile[t] = s_bcast.n
                    dma(p, "sync", moetm[ms][:].rearrange("p (c f) -> p c f", c=NC),
                        x1tr[:, t, :, :], s_btr,
                        waits=[(s_bcast, s_bcast.n), (s_bpre, mtm_user[ms])], transpose=True)
                    n_tr = s_btr.n
                    btr_tile[t] = n_tr
                    dma(p, "sync", x1resstg[:], x1tm_s[seq, tok0:tok0 + 128, :], s_bx1r,
                        waits=[(s_bpre, s_bpre.n)])
                    p.op("vector", lambda e, ms=ms: e.tensor_tensor(out=pre2[:], in0=moetm[ms][:],
                                                                    in1=x1resstg[:], op=ALU.add),
                         waits=[(s_btr, n_tr), (s_bx1r, s_bx1r.n), (s_bx2, s_bx2.n)], inc=s_bpre)
                    mtm_user[ms] = s_bpre.n
                    p.op("vector", lambda e: e.reduce_sum(out=b_ns[:], in_=pre2[:], axis=AX.X, negate=True),
                         waits=[(s_bpre, s_bpre.n), (s_bt1, s_bt1.n)], inc=s_bns)
                    p.op("scalar", lambda e: e.activation(out=x2st[:], in_=pre2[:],
                                                          func=AF.Square, accum_out=b_sq[:]),
                         waits=[(s_bpre, s_bpre.n), (s_bt3, s_bt3.n), (s_out, s_out.n)])
                    p.op("scalar", lambda e: e.copy(out=b_sq[:], in_=b_sq[:]), waits=[], inc=s_bsq)
                    p.op("vector", lambda e: e.tensor_scalar(out=b_nm[:], in0=b_ns[:], scalar1=1.0 / Dm,
                                                             scalar2=None, op0=ALU.mult),
                         waits=[(s_bns, s_bns.n), (s_bx2, s_bx2.n)], inc=s_bt1)
                    p.op("vector", lambda e: e.tensor_tensor(out=b_m2[:], in0=b_nm[:], in1=b_nm[:], op=ALU.mult),
                         waits=[(s_bt1, s_bt1.n), (s_bt3, s_bt3.n)], inc=s_bt2)
                    p.op("vector", lambda e: e.tensor_scalar(out=b_var[:], in0=b_sq[:], scalar1=1.0 / Dm,
                                                             scalar2=b_m2[:], op0=ALU.mult, op1=ALU.subtract),
                         waits=[(s_bsq, s_bsq.n), (s_bt2, s_bt2.n), (s_bst_, s_bst_.n)], inc=s_bt3)
                    p.op("scalar", lambda e: e.activation(out=b_std[:], in_=b_var[:], func=AF.Sqrt, bias=eps_col[:]),
                         waits=[(s_bt3, s_bt3.n), (s_brs, s_brs.n)], inc=s_bst_)
                    p.op("vector", lambda e: e.reciprocal(out=b_rs[:], in_=b_std[:]),
                         waits=[(s_bst_, s_bst_.n), (s_bx2, s_bx2.n)], inc=s_brs)
                    p.op("vector", lambda e: e.tensor_scalar(out=x2st[:], in0=pre2[:], scalar1=b_nm[:],
                                                             scalar2=b_rs[:], op0=ALU.add, op1=ALU.mult),
                         waits=[(s_brs, s_brs.n), (s_bsq, s_bsq.n), (s_out, s_out.n)], inc=s_bx2)
                    dma(p, "sync", out[seq, tok0:tok0 + 128, :], x2st[:], s_out,
                        waits=[(s_bx2, s_bx2.n)])

            for blk in range(NBLK):
                stage_b(blk)

            if x1dump is not None:
                dma(p, "sync", x1dump[:, :, :], x1tm_s[:, :, :], s_out, waits=[(s_x1w, n_x1w_all)])
            n_out = s_out.n
            p.op("sync", lambda e: e.wait_ge(s_out.h, n_out), waits=[])

            with nc.Block() as block:
                def emit(eng_name):
                    def run(eng):
                        for fn, ws, tgt in p.q[eng_name]:
                            for s, v in ws:
                                eng.wait_ge(s.h, v)
                            ins = fn(eng)
                            if tgt is not None:
                                s, k = tgt
                                ins.then_inc(s.h, k)
                    return run

                block.tensor(emit("tensor"))
                block.vector(emit("vector"))
                block.scalar(emit("scalar"))
                block.gpsimd(emit("gpsimd"))
                block.sync(emit("sync"))

    return nc


def _host_tables():
    inv = np.power(10000.0, -2.0 * np.arange(HD // 2, dtype=np.float64) / HD)
    t = np.arange(T, dtype=np.float64)
    ang = t[None, :] * inv[:, None]
    cos = np.cos(ang)
    sin = np.sin(ang)
    cosT = np.concatenate([cos, cos], axis=0).astype(np.float32)
    sinT = np.concatenate([-sin, sin], axis=0).astype(np.float32)
    return cosT, sinT, (cosT * SCALE).astype(np.float32), (sinT * SCALE).astype(np.float32)


_NC_CACHE = {}


def kernel(L0_attn_x, L0_attn_Wq, L0_attn_Wk, L0_attn_Wv, L0_attn_Wo,
           L0_ffn_w_router, L0_ffn_wg, L0_ffn_wu, L0_ffn_wd,
           L0_ffn_sw_gate, L0_ffn_sw_up, L0_ffn_sw_down, L0_ffn_gate_w):
    x = np.ascontiguousarray(np.asarray(L0_attn_x, dtype=np.float32))
    cosT, sinT, cosQ, sinQ = _host_tables()
    if "nc" not in _NC_CACHE:
        _NC_CACHE["nc"] = build_program()
    nc = _NC_CACHE["nc"]
    common = {
        "wq": np.ascontiguousarray(np.asarray(L0_attn_Wq, np.float32)),
        "wk": np.ascontiguousarray(np.asarray(L0_attn_Wk, np.float32)),
        "wv": np.ascontiguousarray(np.asarray(L0_attn_Wv, np.float32)),
        "wo": np.ascontiguousarray(np.asarray(L0_attn_Wo, np.float32)),
        "wrouter": np.ascontiguousarray(np.asarray(L0_ffn_w_router, np.float32)),
        "wg": np.ascontiguousarray(np.asarray(L0_ffn_wg, np.float32).reshape(E * II, Dm)),
        "wu": np.ascontiguousarray(np.asarray(L0_ffn_wu, np.float32).reshape(E * II, Dm)),
        "wd": np.ascontiguousarray(np.asarray(L0_ffn_wd, np.float32).reshape(E * Dm, II)),
        "swg": np.ascontiguousarray(np.asarray(L0_ffn_sw_gate, np.float32)),
        "swu": np.ascontiguousarray(np.asarray(L0_ffn_sw_up, np.float32)),
        "swd": np.ascontiguousarray(np.asarray(L0_ffn_sw_down, np.float32)),
        "gatew": np.ascontiguousarray(np.asarray(L0_ffn_gate_w, np.float32)),
        "cosT": cosT,
        "sinT": sinT,
        "cosQ": cosQ,
        "sinQ": sinQ,
    }
    in_maps = []
    for c in range(NCORES):
        m = dict(common)
        m["x"] = x[c * SEQ:(c + 1) * SEQ]
        in_maps.append(m)
    res = run_bass_kernel_spmd(nc, in_maps, core_ids=list(range(NCORES)))
    return np.concatenate([np.asarray(r["out"], np.float32) for r in res.results], axis=0)


# version 18
# speedup vs baseline: 1.0529x; 1.0456x over previous
import numpy as np
import ml_dtypes
import concourse.bass as bass
import concourse.mybir as mybir
from concourse.bass_utils import run_bass_kernel_spmd
from contextlib import ExitStack

F32 = mybir.dt.float32
BF16 = mybir.dt.bfloat16
AX = mybir.AxisListType
ALU = mybir.AluOpType
AF = mybir.ActivationFunctionType

B, T, Dm = 16, 2048, 2048
NH, NKV, HD = 16, 4, 128
E, II, IS = 8, 1024, 4096
NCORES = 8
SEQ = B // NCORES
NG = 4
GT = T // NG
NTT = T // 128
NC = Dm // 128
NIS = IS // 128
NIE = II // 128
BT = 1024
NBLK = (SEQ * T) // BT
SCALE = 1.0 / np.sqrt(HD)
NEG = -1.0e9

ENGINES = ("tensor", "vector", "scalar", "gpsimd", "sync")


class Sm:
    __slots__ = ("h", "n", "name")

    def __init__(self, h, name):
        self.h = h
        self.n = 0
        self.name = name


class Plan:
    def __init__(self):
        self.q = {e: [] for e in ENGINES}
        self.maxw = {e: {} for e in ENGINES}
        self.last_inc = {e: None for e in ENGINES}

    def op(self, eng, fn, waits=(), inc=None, k=1):
        ws = []
        seen = self.maxw[eng]
        for s, v in waits:
            if v <= 0:
                continue
            if seen.get(id(s), -1) >= v:
                continue
            seen[id(s)] = v
            ws.append((s, v))
        tgt = None
        if inc is not None:
            inc.n += k
            tgt = (inc, k)
            self.last_inc[eng] = (inc, inc.n)
        self.q[eng].append((fn, ws, tgt))

    def fence(self, engs=ENGINES):
        return [self.last_inc[e] for e in engs if self.last_inc[e] is not None]


def dma(plan, q, out, in_, sem, waits=(), transpose=False, slow=False):
    def fn(e, o=out, i=in_, t=transpose, sl=slow):
        return e.dma_start(out=o, in_=i, transpose=t, allow_slow_non_contiguous=sl)
    plan.op(q, fn, waits, sem, 16)


def build_program(debug_x1=False, a2_depth=2):
    nc = bass.Bass("TRN2", target_bir_lowering=False)
    p = Plan()

    xin = nc.dram_tensor("x", [SEQ, T, Dm], F32, kind="ExternalInput").ap()
    wq = nc.dram_tensor("wq", [Dm, Dm], F32, kind="ExternalInput").ap()
    wk = nc.dram_tensor("wk", [Dm, NKV * HD], F32, kind="ExternalInput").ap()
    wv = nc.dram_tensor("wv", [Dm, NKV * HD], F32, kind="ExternalInput").ap()
    wo = nc.dram_tensor("wo", [Dm, Dm], F32, kind="ExternalInput").ap()
    wrouter = nc.dram_tensor("wrouter", [E, Dm], F32, kind="ExternalInput").ap()
    wg_in = nc.dram_tensor("wg", [E * II, Dm], F32, kind="ExternalInput").ap()
    wu_in = nc.dram_tensor("wu", [E * II, Dm], F32, kind="ExternalInput").ap()
    wd_in = nc.dram_tensor("wd", [E * Dm, II], F32, kind="ExternalInput").ap()
    swg_in = nc.dram_tensor("swg", [IS, Dm], F32, kind="ExternalInput").ap()
    swu_in = nc.dram_tensor("swu", [IS, Dm], F32, kind="ExternalInput").ap()
    swd_in = nc.dram_tensor("swd", [Dm, IS], F32, kind="ExternalInput").ap()
    gatew = nc.dram_tensor("gatew", [1, Dm], F32, kind="ExternalInput").ap()
    cos_in = nc.dram_tensor("cosT", [HD, T], F32, kind="ExternalInput").ap()
    sin_in = nc.dram_tensor("sinT", [HD, T], F32, kind="ExternalInput").ap()
    cosq_in = nc.dram_tensor("cosQ", [HD, T], F32, kind="ExternalInput").ap()
    sinq_in = nc.dram_tensor("sinQ", [HD, T], F32, kind="ExternalInput").ap()
    out = nc.dram_tensor("out", [SEQ, T, Dm], F32, kind="ExternalOutput").ap()
    x1dump = nc.dram_tensor("x1dump", [SEQ, T, Dm], BF16, kind="ExternalOutput").ap() if debug_x1 else None

    wkb = nc.dram_tensor("wkb", [Dm, NKV * HD], BF16, kind="Internal").ap()
    wvb = nc.dram_tensor("wvb", [Dm, NKV * HD], BF16, kind="Internal").ap()
    wqb = nc.dram_tensor("wqb", [Dm, Dm], BF16, kind="Internal").ap()
    wob = nc.dram_tensor("wob", [Dm, Dm], BF16, kind="Internal").ap()
    sgn = nc.dram_tensor("sgn", [IS, Dm], BF16, kind="Internal").ap()
    sun = nc.dram_tensor("sun", [IS, Dm], BF16, kind="Internal").ap()
    wgn = nc.dram_tensor("wgn", [E * II, Dm], BF16, kind="Internal").ap()
    wun = nc.dram_tensor("wun", [E * II, Dm], BF16, kind="Internal").ap()
    sdn = nc.dram_tensor("sdn", [Dm, IS], BF16, kind="Internal").ap()
    wdn = nc.dram_tensor("wdn", [E * Dm, II], BF16, kind="Internal").ap()
    x1tm_s = nc.dram_tensor("x1tm_s", [SEQ, T, Dm], BF16, kind="Internal").ap()

    st = ExitStack()

    def sem(name):
        return Sm(st.enter_context(nc.semaphore(name)), name)

    with st:
        s_tbl = sem("tbl")

        ident = st.enter_context(nc.sbuf_tensor("ident", [128, 128], BF16))
        dmask = st.enter_context(nc.sbuf_tensor("dmask", [128, 128], BF16))
        sel = st.enter_context(nc.sbuf_tensor("sel", [E, E * 128], BF16))
        ones1 = st.enter_context(nc.sbuf_tensor("ones1", [1, 128], BF16))
        wrt = st.enter_context(nc.sbuf_tensor("wrt", [128, NC * E], BF16))
        gwt = st.enter_context(nc.sbuf_tensor("gwt", [128, NC], BF16))
        eps_col = st.enter_context(nc.sbuf_tensor("eps_col", [128, 1], F32))

        def gen_consts(gp):
            gp.memset(ident[:], 0.0)
            gp.affine_select(out=ident[:], in_=ident[:], compare_op=ALU.not_equal,
                             fill=1.0, base=0, pattern=[[-1, 128]], channel_multiplier=1)
            gp.memset(dmask[:], 0.0)
            gp.affine_select(out=dmask[:], in_=dmask[:], compare_op=ALU.is_ge,
                             fill=NEG, base=0, pattern=[[-1, 128]], channel_multiplier=1)
            gp.memset(sel[:], 0.0)
            for e_ in range(E):
                gp.affine_select(out=sel[:, e_ * 128:(e_ + 1) * 128], in_=sel[:, e_ * 128:(e_ + 1) * 128],
                                 compare_op=ALU.not_equal, fill=1.0, base=-e_,
                                 pattern=[[0, 128]], channel_multiplier=1)
            gp.memset(ones1[:], 1.0)
            return gp.memset(eps_col[:], 1e-5)

        p.op("gpsimd", gen_consts, inc=s_tbl)
        for c in range(NC):
            dma(p, "gpsimd", wrt[:, c * E:(c + 1) * E],
                wrouter[:, c * 128:(c + 1) * 128].rearrange("e p -> p e"), s_tbl, slow=True)
            dma(p, "gpsimd", gwt[:, c:c + 1],
                gatew[:, c * 128:(c + 1) * 128].rearrange("a p -> p a"), s_tbl, slow=True)

        s_pc = sem("pc")
        s_pck = sem("pck")
        s_pcv = sem("pcv")
        s_pcq = sem("pcq")
        s_pco = sem("pco")
        prep_jobs = []

        def add_cast(src2d, dst2d, nrows, rows_per, psem=None):
            for r0 in range(0, nrows, rows_per):
                prep_jobs.append((src2d[r0:r0 + rows_per, :], dst2d[r0:r0 + rows_per, :], psem or s_pc))

        add_cast(wk, wkb, Dm, 512, s_pck)
        N_WK = 4
        add_cast(wv, wvb, Dm, 512, s_pcv)
        N_WV = 4
        add_cast(wq, wqb, Dm, 256, s_pcq)
        N_WQ = 8
        add_cast(wo, wob, Dm, 256, s_pco)
        N_ATTN_PREP = 8
        add_cast(swg_in, sgn, IS, 256)
        add_cast(swu_in, sun, IS, 256)
        add_cast(wg_in, wgn, E * II, 256)
        add_cast(wu_in, wun, E * II, 256)
        add_cast(swd_in, sdn, Dm, 128)
        add_cast(wd_in, wdn, E * Dm, 512)
        NPREP = len(prep_jobs) - 24
        NPREP_ALL = len(prep_jobs)

        prep_state = {"i": 0}

        def emit_prep(njobs):
            for _ in range(njobs):
                i = prep_state["i"]
                if i >= NPREP_ALL:
                    return
                prep_state["i"] += 1
                src, dst, psem = prep_jobs[i]
                dma(p, "gpsimd", dst, src, psem)

        ast = ExitStack()
        with ast:
            cosT = ast.enter_context(nc.sbuf_tensor("cos_sb", [128, T], BF16))
            sinT = ast.enter_context(nc.sbuf_tensor("sin_sb", [128, T], BF16))
            cosQ = ast.enter_context(nc.sbuf_tensor("cosq_sb", [128, T], BF16))
            sinQ = ast.enter_context(nc.sbuf_tensor("sinq_sb", [128, T], BF16))
            k_rope = ast.enter_context(nc.sbuf_tensor("k_rope", [128, NKV * T], BF16))
            v_tm = ast.enter_context(nc.sbuf_tensor("v_tm", [128, NTT * 512], BF16))
            q_rope = ast.enter_context(nc.sbuf_tensor("q_rope", [128, NC * GT], BF16))
            xattn = ast.enter_context(nc.sbuf_tensor("xattn", [128, NC * GT], BF16))
            xofm = ast.enter_context(nc.sbuf_tensor("xofm", [128, NC * GT], BF16))
            xstg = ast.enter_context(nc.sbuf_tensor("xstg", [128, Dm], BF16))
            xtmp = ast.enter_context(nc.sbuf_tensor("xtmp", [128, Dm], BF16))
            wsl = [ast.enter_context(nc.sbuf_tensor(f"wsl{i}", [128, NC * 512], BF16)) for i in range(2)]
            rraw = [ast.enter_context(nc.sbuf_tensor(f"rraw{i}", [128, GT], F32)) for i in range(2)]
            rsh = [ast.enter_context(nc.sbuf_tensor(f"rsh{i}", [128, GT], F32)) for i in range(2)]
            prow = ast.enter_context(nc.sbuf_tensor("prow", [128, T], BF16))
            atb = ast.enter_context(nc.sbuf_tensor("atb", [128, T], BF16))
            ygrp = ast.enter_context(nc.sbuf_tensor("ygrp", [128, NH * 512], BF16))
            xres = ast.enter_context(nc.sbuf_tensor("xres", [128, Dm], F32))
            pre = ast.enter_context(nc.sbuf_tensor("pre", [128, Dm], F32))
            attn_tm = ast.enter_context(nc.sbuf_tensor("attn_tm", [128, Dm], BF16))
            x1st = ast.enter_context(nc.sbuf_tensor("x1st", [128, Dm], BF16))
            col_m = ast.enter_context(nc.sbuf_tensor("col_m", [128, 8], F32))
            col_b = ast.enter_context(nc.sbuf_tensor("col_b", [128, 2], F32))
            col_se = ast.enter_context(nc.sbuf_tensor("col_se", [128, 2], F32))
            col_rc = ast.enter_context(nc.sbuf_tensor("col_rc", [128, 2], F32))
            ln_ns = ast.enter_context(nc.sbuf_tensor("ln_ns", [128, 1], F32))
            ln_sq = ast.enter_context(nc.sbuf_tensor("ln_sq", [128, 1], F32))
            ln_nm = ast.enter_context(nc.sbuf_tensor("ln_nm", [128, 1], F32))
            ln_m2 = ast.enter_context(nc.sbuf_tensor("ln_m2", [128, 1], F32))
            ln_var = ast.enter_context(nc.sbuf_tensor("ln_var", [128, 1], F32))
            ln_std = ast.enter_context(nc.sbuf_tensor("ln_std", [128, 1], F32))
            ln_rs = ast.enter_context(nc.sbuf_tensor("ln_rs", [128, 1], F32))

            ps_row = ast.enter_context(nc.psum_tensor("ps_row", [128, T], F32))
            ps_y0 = ast.enter_context(nc.psum_tensor("ps_y0", [128, 512], F32))
            ps_y = [ps_y0, ps_y0]
            ps_a0 = ast.enter_context(nc.psum_tensor("ps_a0", [128, 512], F32))
            ps_at = ast.enter_context(nc.psum_tensor("ps_at", [128, T], BF16))
            ps_a = [ps_a0, ps_y0]

            s_tld = sem("tld")
            s_xld = sem("xld")
            s_xtr0 = sem("xtr0")
            s_xtr = sem("xtr")
            s_wld = sem("wld")
            s_pa = sem("pa")
            s_pev = sem("pev")
            s_rsh = sem("rsh")
            s_rt1 = sem("rt1")
            s_rt2 = sem("rt2")
            s_kq = sem("kq")
            s_v = sem("v")
            s_scr = sem("scr")
            s_msk = sem("msk")
            s_max = sem("max")
            s_bias = sem("bias")
            s_exp = sem("exp")
            s_rcp = sem("rcp")
            s_psc = sem("psc")
            s_att = sem("att")
            s_atb = sem("atb")
            s_av = sem("av")
            s_yev = sem("yev")
            s_oev = sem("oev")
            s_otr = sem("otr")
            s_xr = sem("xr")
            s_lpre = sem("lpre")
            s_lns = sem("lns")
            s_lsq = sem("lsq")
            s_lt1 = sem("lt1")
            s_lt2 = sem("lt2")
            s_lt3 = sem("lt3")
            s_lst = sem("lst")
            s_lrs = sem("lrs")
            s_x1 = sem("x1")
            s_x1w = sem("x1w")

            dma(p, "gpsimd", cosT[:], cos_in, s_tld)
            dma(p, "gpsimd", sinT[:], sin_in, s_tld)
            dma(p, "gpsimd", cosQ[:], cosq_in, s_tld)
            dma(p, "gpsimd", sinQ[:], sinq_in, s_tld)

            xattn_v = xattn[:].rearrange("p (c t) -> p c t", c=NC)
            xofm_t4 = xofm[:].rearrange("p (t c f) -> p t c f", t=NG, c=NC)

            wsl_user = [0, 0]
            slab_ctr = [0]
            rope_user = [0, 0]
            rope_ctr = [0]
            row_user = [0, 0]
            prow_user = [0, 0]
            psat_user = [0, 0]
            atb_user = [0, 0]
            se_user = [0, 0]
            y_user = [0, 0]
            a2_count = [0]
            yq_count = [0]
            pev_gate = [0]
            oev_gate = [0]
            xcopy_gate = [0]

            def rope_chunk(seq, g, psum, dst_col_ap, ctab, stab):
                sl = rope_ctr[0] % 2
                rope_ctr[0] += 1
                rw, rs = rraw[sl], rsh[sl]
                p.op("scalar", lambda e, ps=psum, rw=rw: e.copy(out=rw[:], in_=ps[:]),
                     waits=[(s_pa, s_pa.n), (s_kq, rope_user[sl])], inc=s_pev)
                n_ev = s_pev.n
                dma(p, "sync", rs[0:64, :], rw[64:128, :], s_rsh,
                    waits=[(s_pev, n_ev), (s_kq, rope_user[sl])])
                dma(p, "sync", rs[64:128, :], rw[0:64, :], s_rsh, waits=[(s_pev, n_ev)])
                n_sh = s_rsh.n
                p.op("vector", lambda e, sl_=(g * GT), rw=rw: e.tensor_tensor(
                    out=rw[:], in0=rw[:], in1=ctab[:, sl_:sl_ + GT], op=ALU.mult),
                    waits=[(s_pev, n_ev), (s_rsh, n_sh), (s_tld, 64)], inc=s_rt1)
                p.op("vector", lambda e, sl_=(g * GT), rs=rs: e.tensor_tensor(
                    out=rs[:], in0=rs[:], in1=stab[:, sl_:sl_ + GT], op=ALU.mult),
                    waits=[(s_rsh, n_sh), (s_tld, 64)], inc=s_rt2)
                p.op("vector", lambda e, d=dst_col_ap, rw=rw, rs=rs: e.tensor_tensor(
                    out=d, in0=rw[:], in1=rs[:], op=ALU.add),
                    waits=[(s_rt1, s_rt1.n), (s_rt2, s_rt2.n), (s_scr, s_scr.n)], inc=s_kq)
                rope_user[sl] = s_kq.n

            def load_slab(src_fn, ncols, psem, prep_n):
                slot = slab_ctr[0] % 2
                slab_ctr[0] += 1
                for c in range(NC):
                    dma(p, "gpsimd", wsl[slot][:, c * 512:c * 512 + ncols], src_fn(c), s_wld,
                        waits=[(s_pa, wsl_user[slot]), (psem, 16 * prep_n)])
                return slot, s_wld.n

            def a1_xpath(seq, g):
                for t in range(NG):
                    tok0 = g * GT + t * 128
                    dma(p, "gpsimd", xstg[:], xin[seq, tok0:tok0 + 128, :], s_xld,
                        waits=[(s_xtr0, s_xtr0.n)])
                    dma(p, "sync", xtmp[:].rearrange("p (c f) -> p c f", c=NC),
                        xstg[:].rearrange("p (c f) -> p c f", c=NC), s_xtr0,
                        waits=[(s_xld, s_xld.n), (s_xtr, s_xtr.n)], transpose=True)
                    p.op("scalar", lambda e, t=t: e.copy(out=xattn_v[:, :, t * 128:(t + 1) * 128],
                                                         in_=xtmp[:].rearrange("p (c f) -> p c f", c=NC)),
                         waits=[(s_xtr0, s_xtr0.n), (s_pa, xcopy_gate[0])], inc=s_xtr)

            def stage_a1(seq, g):
                n_xtr = s_xtr.n

                slot, nw = load_slab(lambda c: wkb[c * 128:(c + 1) * 128, :], 512, s_pck, N_WK)
                for o in range(NKV):
                    pb = ps_a[o % 2]
                    def mmk(e, o=o, pb=pb, slot=slot):
                        ins = None
                        for c in range(NC):
                            ins = e.matmul(pb[:], wsl[slot][:, c * 512 + o * 128: c * 512 + (o + 1) * 128],
                                           xattn_v[:, c, :], start=(c == 0), stop=(c == NC - 1))
                        return ins
                    p.op("tensor", mmk,
                         waits=[(s_wld, nw), (s_xtr, n_xtr), (s_pev, s_pev.n), (s_v, s_v.n),
                                (s_oev, s_oev.n), (s_yev, s_yev.n), (s_tbl, 1)], inc=s_pa)
                    rope_chunk(seq, g, pb, k_rope[:, o * T + g * GT: o * T + (g + 1) * GT], cosT, sinT)
                wsl_user[slot] = s_pa.n
                slot, nw = load_slab(lambda c: wvb[c * 128:(c + 1) * 128, :], 512, s_pcv, N_WV)
                for t in range(NG):
                    pb = ps_a[t % 2]
                    def mmv(e, t=t, pb=pb, slot=slot):
                        ins = None
                        for c in range(NC):
                            ins = e.matmul(pb[:], xattn_v[:, c, t * 128:(t + 1) * 128],
                                           wsl[slot][:, c * 512:(c + 1) * 512],
                                           start=(c == 0), stop=(c == NC - 1))
                        return ins
                    p.op("tensor", mmv, waits=[(s_wld, nw), (s_pev, s_pev.n), (s_v, s_v.n),
                                               (s_oev, s_oev.n), (s_yev, s_yev.n)], inc=s_pa)
                    tt = g * NG + t
                    p.op("scalar", lambda e, pb=pb, tt=tt: e.copy(
                        out=v_tm[:, tt * 512:(tt + 1) * 512], in_=pb[:]),
                        waits=[(s_pa, s_pa.n), (s_av, s_av.n)], inc=s_v)
                wsl_user[slot] = s_pa.n
                for qtr in range(4):
                    slot, nw = load_slab(lambda c, qtr=qtr: wqb[c * 128:(c + 1) * 128,
                                                               qtr * 512:(qtr + 1) * 512], 512, s_pcq, N_WQ)
                    for oo in range(4):
                        o = qtr * 4 + oo
                        pb = ps_a[o % 2]
                        def mmq(e, oo=oo, slot=slot, pb=pb):
                            ins = None
                            for c in range(NC):
                                ins = e.matmul(pb[:], wsl[slot][:, c * 512 + oo * 128: c * 512 + (oo + 1) * 128],
                                               xattn_v[:, c, :], start=(c == 0), stop=(c == NC - 1))
                            return ins
                        p.op("tensor", mmq, waits=[(s_wld, nw), (s_pev, s_pev.n), (s_v, s_v.n),
                                                   (s_oev, s_oev.n), (s_yev, s_yev.n)], inc=s_pa)
                        rope_chunk(seq, g, pb, q_rope[:, o * GT:(o + 1) * GT], cosQ, sinQ)
                    wsl_user[slot] = s_pa.n
                xcopy_gate[0] = s_pa.n
                pev_gate[0] = s_pev.n

            def stage_a2(seq, i, h):
                k = a2_count[0]
                a2_count[0] += 1
                g = h // 4
                w = (i // 4) * 512 + (i % 4 + 1) * 128
                nt = w // 128
                if i < 8:
                    halves = (k % 2,)
                    off = (k % 2) * 1024
                else:
                    halves = (0, 1)
                    off = 0
                sl = halves[0] if len(halves) == 1 else 0
                cs = sl if len(halves) == 1 else 0
                own = halves
                row_wait = max(row_user[h_] for h_ in own)
                prow_wait = max(prow_user[h_] for h_ in own)
                psat_wait = max(psat_user[h_] for h_ in own)
                atb_wait = max(atb_user[h_] for h_ in own)
                se_wait = max(se_user[h_] for h_ in own)
                pr_ap = prow[:, off:off + w]
                at_ap = atb[:, off:off + w]
                cm = col_m[:, cs * 4:cs * 4 + 4]
                cb = col_b[:, cs:cs + 1]
                cse = col_se[:, cs:cs + 1]
                crc = col_rc[:, cs:cs + 1]

                nblk = (w + 511) // 512
                for kb in range(nblk):
                    bw = min(512, w - kb * 512)
                    def mblk(e, i=i, h=h, g=g, off=off, kb=kb, bw=bw, last=(kb == nblk - 1)):
                        ins = e.matmul(ps_row[:, off + kb * 512: off + kb * 512 + bw],
                                       q_rope[:, h * GT + (i % 4) * 128: h * GT + (i % 4 + 1) * 128],
                                       k_rope[:, g * T + kb * 512: g * T + kb * 512 + bw],
                                       start=True, stop=not last)
                        if last:
                            ins = e.matmul(ps_row[:, off + w - 128: off + w], ident[:], dmask[:],
                                           start=False, stop=True)
                        return ins
                    p.op("tensor", mblk,
                         waits=([(s_kq, (seq * NG + i // 4 + 1) * 20), (s_exp, row_wait),
                                 (s_tbl, 1)] if kb == 0 else []),
                         inc=s_scr)
                    n_scr = s_scr.n
                    p.op("vector", lambda e, off=off, kb=kb, bw=bw, cm=cm: e.reduce_max(
                        out=cm[:, kb:kb + 1], in_=ps_row[:, off + kb * 512: off + kb * 512 + bw], axis=AX.X),
                        waits=[(s_scr, n_scr), (s_bias, s_bias.n)], inc=s_max)
                p.op("vector", lambda e, cm=cm, cb=cb, nblk=nblk: e.reduce_max(
                    out=cb, in_=cm[:, 0:nblk], axis=AX.X, negate=True),
                    waits=[(s_max, s_max.n), (s_exp, s_exp.n)], inc=s_bias)
                p.op("scalar", lambda e, w=w, off=off, pr=pr_ap, cb=cb, cse=cse: e.activation(
                    out=pr, in_=ps_row[:, off:off + w], func=AF.Exp,
                    bias=cb, scale=1.0, accum_out=cse),
                    waits=[(s_bias, s_bias.n), (s_att, prow_wait), (s_rcp, se_wait)])
                p.op("scalar", lambda e, cse=cse: e.copy(out=cse, in_=cse), waits=[], inc=s_exp)
                for h_ in own:
                    row_user[h_] = s_exp.n
                p.op("vector", lambda e, cse=cse, crc=crc: e.reciprocal(out=crc, in_=cse),
                     waits=[(s_exp, s_exp.n)], inc=s_rcp)
                for h_ in own:
                    se_user[h_] = s_rcp.n
                p.op("vector", lambda e, pr=pr_ap, crc=crc: e.tensor_scalar(
                    out=pr, in0=pr, scalar1=crc, scalar2=None, op0=ALU.mult),
                    waits=[(s_rcp, s_rcp.n), (s_att, prow_wait)], inc=s_psc)
                n_psc = s_psc.n

                def emit_t(i=i, nt=nt, off=off, own=own, w=w, n_psc=n_psc,
                           psat_wait=psat_wait, atb_wait=atb_wait):
                    def mmt(e):
                        ins = None
                        for j in range(nt):
                            ins = e.transpose(ps_at[:, off + j * 128: off + (j + 1) * 128],
                                              prow[:, off + j * 128: off + (j + 1) * 128], ident[:])
                        return ins
                    p.op("tensor", mmt,
                         waits=[(s_psc, n_psc), (s_atb, psat_wait)], inc=s_att)
                    for h_ in own:
                        prow_user[h_] = s_att.n
                    p.op("vector", lambda e, off=off, w=w: e.tensor_copy(
                        out=atb[:, off:off + w], in_=ps_at[:, off:off + w]),
                        waits=[(s_att, s_att.n), (s_av, atb_wait)], inc=s_atb)
                    for h_ in own:
                        psat_user[h_] = s_atb.n
                    return s_atb.n

                def emit_av(seq=seq, i=i, h=h, g=g, nt=nt, off=off, own=own):
                    qd = yq_count[0]
                    pyb = ps_y[qd % 2]
                    def mmav(e, g=g, h=h, nt=nt, off=off, pyb=pyb):
                        ins = None
                        for j in range(nt):
                            ins = e.matmul(pyb[:, (h % 4) * 128:(h % 4 + 1) * 128],
                                           v_tm[:, j * 512 + g * 128: j * 512 + (g + 1) * 128],
                                           atb[:, off + j * 128: off + (j + 1) * 128],
                                           start=(j == 0), stop=(j == nt - 1))
                        return ins
                    p.op("tensor", mmav,
                         waits=[(s_atb, s_atb.n), (s_v, (seq * NG + i // 4 + 1) * NG),
                                (s_yev, y_user[qd % 2]), (s_pev, pev_gate[0]), (s_oev, oev_gate[0])],
                         inc=s_av)
                    for h_ in own:
                        atb_user[h_] = s_av.n
                    if h % 4 == 3:
                        hq = h - 3
                        def yev(e, i=i, hq=hq, pyb=pyb):
                            og = ygrp[:].rearrange("p (hh t) -> p hh t", hh=NH)
                            return e.copy(out=og[:, hq:hq + 4, (i % 4) * 128:(i % 4 + 1) * 128],
                                          in_=pyb[:].rearrange("p (hh t) -> p hh t", hh=4))
                        p.op("scalar", yev, waits=[(s_av, s_av.n), (s_pa, s_pa.n)], inc=s_yev)
                        y_user[0] = y_user[1] = s_yev.n
                        yq_count[0] += 1
                return emit_t, emit_av

            def stage_a3(seq, g):
                for qtr in range(4):
                    slot, nw = load_slab(lambda c, qtr=qtr: wob[c * 128:(c + 1) * 128,
                                                               qtr * 512:(qtr + 1) * 512], 512, s_pco, N_ATTN_PREP)
                    for ff in range(4):
                        f = qtr * 4 + ff
                        pb = ps_a[f % 2]
                        def mmo(e, ff=ff, slot=slot, pb=pb):
                            ins = None
                            for c in range(NC):
                                ins = e.matmul(pb[:], wsl[slot][:, c * 512 + ff * 128: c * 512 + (ff + 1) * 128],
                                               ygrp[:, c * 512:(c + 1) * 512],
                                               start=(c == 0), stop=(c == NC - 1))
                            return ins
                        p.op("tensor", mmo,
                             waits=[(s_wld, nw), (s_yev, s_yev.n), (s_pev, s_pev.n),
                                    (s_oev, s_oev.n), (s_v, s_v.n)],
                             inc=s_pa)
                        p.op("scalar", lambda e, pb=pb, f=f: e.copy(
                            out=xofm_t4[:, :, f, :],
                            in_=pb[:].rearrange("p (t f) -> p t f", t=NG)),
                            waits=[(s_pa, s_pa.n), (s_otr, s_otr.n)], inc=s_oev)
                    wsl_user[slot] = s_pa.n
                oev_gate[0] = s_oev.n
                for t in range(NG):
                    tok0 = g * GT + t * 128
                    dma(p, "sync", attn_tm[:].rearrange("p (c f) -> p c f", c=NC),
                        xofm_t4[:, t, :, :], s_otr,
                        waits=[(s_oev, s_oev.n), (s_lpre, s_lpre.n)], transpose=True)
                    dma(p, "sync", xres[:], xin[seq, tok0:tok0 + 128, :], s_xr,
                        waits=[(s_lsq, s_lsq.n)])
                    p.op("vector", lambda e: e.tensor_tensor(out=pre[:], in0=attn_tm[:], in1=xres[:], op=ALU.add),
                         waits=[(s_otr, s_otr.n), (s_xr, s_xr.n), (s_x1, s_x1.n)], inc=s_lpre)
                    p.op("vector", lambda e: e.reduce_sum(out=ln_ns[:], in_=pre[:], axis=AX.X, negate=True),
                         waits=[(s_lpre, s_lpre.n), (s_lt1, s_lt1.n)], inc=s_lns)
                    p.op("scalar", lambda e: e.activation(out=xres[:], in_=pre[:], func=AF.Square,
                                                          accum_out=ln_sq[:]),
                         waits=[(s_lpre, s_lpre.n), (s_lt3, s_lt3.n)])
                    p.op("scalar", lambda e: e.copy(out=ln_sq[:], in_=ln_sq[:]), waits=[], inc=s_lsq)
                    p.op("vector", lambda e: e.tensor_scalar(out=ln_nm[:], in0=ln_ns[:], scalar1=1.0 / Dm,
                                                             scalar2=None, op0=ALU.mult),
                         waits=[(s_lns, s_lns.n), (s_x1, s_x1.n)], inc=s_lt1)
                    p.op("vector", lambda e: e.tensor_tensor(out=ln_m2[:], in0=ln_nm[:], in1=ln_nm[:], op=ALU.mult),
                         waits=[(s_lt1, s_lt1.n), (s_lt3, s_lt3.n)], inc=s_lt2)
                    p.op("vector", lambda e: e.tensor_scalar(out=ln_var[:], in0=ln_sq[:], scalar1=1.0 / Dm,
                                                             scalar2=ln_m2[:], op0=ALU.mult, op1=ALU.subtract),
                         waits=[(s_lsq, s_lsq.n), (s_lt2, s_lt2.n), (s_lst, s_lst.n)], inc=s_lt3)
                    p.op("scalar", lambda e: e.activation(out=ln_std[:], in_=ln_var[:], func=AF.Sqrt, bias=eps_col[:]),
                         waits=[(s_lt3, s_lt3.n), (s_lrs, s_lrs.n)], inc=s_lst)
                    p.op("vector", lambda e: e.reciprocal(out=ln_rs[:], in_=ln_std[:]),
                         waits=[(s_lst, s_lst.n), (s_x1, s_x1.n)], inc=s_lrs)
                    p.op("vector", lambda e: e.tensor_scalar(out=x1st[:], in0=pre[:], scalar1=ln_nm[:],
                                                             scalar2=ln_rs[:], op0=ALU.add, op1=ALU.mult),
                         waits=[(s_lrs, s_lrs.n), (s_x1w, s_x1w.n)], inc=s_x1)
                    dma(p, "sync", x1tm_s[seq, tok0:tok0 + 128, :], x1st[:], s_x1w,
                        waits=[(s_x1, s_x1.n)])

            emit_prep(24)
            a1_xpath(0, 0)
            for seq in range(SEQ):
                for g in range(NG):
                    stage_a1(seq, g)
                    emit_prep(3)
                    pend_t = []
                    pend_av = []

                    def flushq():
                        if pend_av:
                            pend_av.pop(0)()
                        while pend_t:
                            pend_t.pop(0)()
                            if pend_av:
                                pend_av.pop(0)()
                        while pend_av:
                            pend_av.pop(0)()

                    for it in range(NG):
                        i = g * NG + it
                        for h in range(NH):
                            if i >= 8:
                                flushq()
                            ft, fav = stage_a2(seq, i, h)
                            if i < 8:
                                if len(pend_av) >= 2:
                                    pend_av.pop(0)()
                                if pend_t:
                                    pend_t.pop(0)()
                                pend_t.append(ft)
                                pend_av.append(fav)
                            else:
                                ft()
                                fav()
                            emit_prep(1)
                        if it == 1:
                            if g + 1 < NG:
                                a1_xpath(seq, g + 1)
                            elif seq + 1 < SEQ:
                                a1_xpath(seq + 1, 0)
                    flushq()
                    stage_a3(seq, g)
                    emit_prep(6)
            emit_prep(10 ** 9)
            n_x1w_all = s_x1w.n
            a_fence = p.fence()

        bst = ExitStack()
        with bst:
            x1f = bst.enter_context(nc.sbuf_tensor("x1f", [128, NC * BT], BF16))
            x1ldstg = bst.enter_context(nc.sbuf_tensor("x1ldstg", [128, Dm], BF16))
            x1resstg = bst.enter_context(nc.sbuf_tensor("x1resstg", [128, Dm], BF16))
            xtmpb = bst.enter_context(nc.sbuf_tensor("xtmpb", [128, Dm], BF16))
            acc = bst.enter_context(nc.sbuf_tensor("acc", [128, NC * BT], F32))
            sfm = bst.enter_context(nc.sbuf_tensor("sfm", [128, 8 * BT], BF16))
            gbuf = [bst.enter_context(nc.sbuf_tensor(f"gbuf{i}", [128, Dm], BF16)) for i in range(2)]
            ubuf = [bst.enter_context(nc.sbuf_tensor(f"ubuf{i}", [128, Dm], BF16)) for i in range(2)]
            dbuf = [bst.enter_context(nc.sbuf_tensor(f"dbuf{i}", [128, 1024], BF16)) for i in range(2)]
            cbs = bst.enter_context(nc.sbuf_tensor("cbs", [128, BT], F32))
            sgb = bst.enter_context(nc.sbuf_tensor("sgb", [128, BT], F32))
            sgrow = bst.enter_context(nc.sbuf_tensor("sgrow", [1, BT], F32))
            sgrow_b = bst.enter_context(nc.sbuf_tensor("sgrow_b", [1, BT], BF16))
            gs = bst.enter_context(nc.sbuf_tensor("gs", [128, BT], BF16))
            usc = bst.enter_context(nc.sbuf_tensor("usc", [128, BT], BF16))
            rp = bst.enter_context(nc.sbuf_tensor("rp", [128, E], F32))
            rm1 = bst.enter_context(nc.sbuf_tensor("rm1", [128, 1], F32))
            rmask = bst.enter_context(nc.sbuf_tensor("rmask", [128, E], F32))
            rp2 = bst.enter_context(nc.sbuf_tensor("rp2", [128, E], F32))
            rm2 = bst.enter_context(nc.sbuf_tensor("rm2", [128, 1], F32))
            rge = bst.enter_context(nc.sbuf_tensor("rge", [128, E], F32))
            rcr = bst.enter_context(nc.sbuf_tensor("rcr", [128, E], F32))
            rden = bst.enter_context(nc.sbuf_tensor("rden", [128, 1], F32))
            rrc = bst.enter_context(nc.sbuf_tensor("rrc", [128, 1], F32))
            rcomb = bst.enter_context(nc.sbuf_tensor("rcomb", [128, E], BF16))
            combfm = bst.enter_context(nc.sbuf_tensor("combfm", [E, BT], BF16))
            moetm = [bst.enter_context(nc.sbuf_tensor(f"moetm{i}", [128, Dm], BF16)) for i in range(4)]
            pre2 = bst.enter_context(nc.sbuf_tensor("pre2", [128, Dm], F32))
            x2st = bst.enter_context(nc.sbuf_tensor("x2st", [128, Dm], F32))
            b_ns = bst.enter_context(nc.sbuf_tensor("b_ns", [128, 1], F32))
            b_sq = bst.enter_context(nc.sbuf_tensor("b_sq", [128, 1], F32))
            b_nm = bst.enter_context(nc.sbuf_tensor("b_nm", [128, 1], F32))
            b_m2 = bst.enter_context(nc.sbuf_tensor("b_m2", [128, 1], F32))
            b_var = bst.enter_context(nc.sbuf_tensor("b_var", [128, 1], F32))
            b_std = bst.enter_context(nc.sbuf_tensor("b_std", [128, 1], F32))
            b_rs = bst.enter_context(nc.sbuf_tensor("b_rs", [128, 1], F32))

            ps_g = bst.enter_context(nc.psum_tensor("ps_g", [128, 1024], F32))
            ps_u = bst.enter_context(nc.psum_tensor("ps_u", [128, 1024], F32))
            ps_d = bst.enter_context(nc.psum_tensor("ps_d", [128, 1024], F32))
            ps_m = bst.enter_context(nc.psum_tensor("ps_m", [128, 512], F32))
            ps_rt = bst.enter_context(nc.psum_tensor("ps_rt", [E, 128], BF16))

            s_bx1 = sem("bx1")
            s_bx1t = sem("bx1t")
            s_bx1c = sem("bx1c")
            s_bwg = sem("bwg")
            s_bwd = sem("bwd")
            s_bg = sem("bg")
            s_bu = sem("bu")
            s_bsil = sem("bsil")
            s_busc = sem("busc")
            s_bgu = sem("bgu")
            s_bd = sem("bd")
            s_bacc = sem("bacc")
            s_brt = sem("brt")
            s_brx = sem("brx")
            s_bcb = sem("bcb")
            s_bsg = sem("bsg")
            s_bcast = sem("bcast")
            s_btr = sem("btr")
            s_bx1r = sem("bx1r")
            s_bpre = sem("bpre")
            s_bns = sem("bns")
            s_bsq = sem("bsq")
            s_bt1 = sem("bt1")
            s_bt2 = sem("bt2")
            s_bt3 = sem("bt3")
            s_bst_ = sem("bst")
            s_brs = sem("brs")
            s_bx2 = sem("bx2")
            s_out = sem("sout")

            x1f_v = x1f[:].rearrange("p (c t) -> p c t", c=NC)
            acc_v = acc[:].rearrange("p (c t) -> p c t", c=NC)
            sfm_v = sfm[:].rearrange("p (c t) -> p c t", c=8)

            first_b = [True]
            wgu_user = [0, 0]
            wd_user = [0, 0]
            gh_user = [0, 0]
            uh_user = [0, 0]
            dh_user = [0, 0]
            ug_ctr = [0]
            dn_ctr = [0]
            mtm_user = [0, 0, 0, 0]
            btr_tile = [0] * 8
            bcast_tile = [0] * 8

            def stage_b(blk):
                seq = blk // 2
                off = (blk % 2) * BT
                fence = a_fence if first_b[0] else []
                base_x1c = s_bx1c.n
                for t in range(8):
                    tok0 = off + t * 128
                    dma(p, "sync", x1ldstg[:], x1tm_s[seq, tok0:tok0 + 128, :], s_bx1,
                        waits=[(s_x1w, n_x1w_all), (s_bx1t, s_bx1t.n)] + fence)
                    fence = []
                    dma(p, "sync", xtmpb[:].rearrange("p (c f) -> p c f", c=NC),
                        x1ldstg[:].rearrange("p (c f) -> p c f", c=NC), s_bx1t,
                        waits=[(s_bx1, s_bx1.n), (s_bx1c, s_bx1c.n)], transpose=True)
                    p.op("scalar", lambda e, t=t: e.copy(out=x1f_v[:, :, t * 128:(t + 1) * 128],
                                                         in_=xtmpb[:].rearrange("p (c f) -> p c f", c=NC)),
                         waits=[(s_bx1t, s_bx1t.n), (s_bg, s_bg.n), (s_bu, s_bu.n), (s_brt, s_brt.n),
                                (s_btr, btr_tile[t]), (s_bcast, bcast_tile[t])],
                         inc=s_bx1c)
                first_b[0] = False
                n_x1t = s_bx1c.n

                for t in range(8):
                    def mmr(e, t=t):
                        ins = None
                        for c in range(NC):
                            ins = e.matmul(ps_m[:, 0:E], x1f_v[:, c, t * 128:(t + 1) * 128],
                                           wrt[:, c * E:(c + 1) * E], start=(c == 0), stop=(c == NC - 1))
                        return ins
                    p.op("tensor", mmr, waits=[(s_bx1c, base_x1c + t + 1), (s_brx, s_brx.n),
                                               (s_tbl, s_tbl.n)], inc=s_brt)
                    p.op("vector", lambda e: e.reduce_max(out=rm1[:], in_=ps_m[:, 0:E], axis=AX.X, negate=True),
                         waits=[(s_brt, s_brt.n)], inc=s_brx)
                    p.op("scalar", lambda e: e.activation(out=rp[:], in_=ps_m[:, 0:E], func=AF.Exp, bias=rm1[:]),
                         waits=[(s_brx, s_brx.n)], inc=s_brx)
                    p.op("vector", lambda e: e.reduce_max(out=rm1[:], in_=rp[:], axis=AX.X),
                         waits=[(s_brx, s_brx.n)], inc=s_brx)
                    p.op("vector", lambda e: e.tensor_scalar(out=rmask[:], in0=rp[:], scalar1=rm1[:],
                                                             scalar2=None, op0=ALU.is_ge),
                         waits=[(s_brx, s_brx.n)], inc=s_brx)
                    p.op("vector", lambda e: e.scalar_tensor_tensor(out=rp2[:], in0=rmask[:], scalar=NEG,
                                                                    in1=rp[:], op0=ALU.mult, op1=ALU.add),
                         waits=[(s_brx, s_brx.n)], inc=s_brx)
                    p.op("vector", lambda e: e.reduce_max(out=rm2[:], in_=rp2[:], axis=AX.X),
                         waits=[(s_brx, s_brx.n)], inc=s_brx)
                    p.op("vector", lambda e: e.tensor_scalar(out=rge[:], in0=rp[:], scalar1=rm2[:],
                                                             scalar2=None, op0=ALU.is_ge),
                         waits=[(s_brx, s_brx.n)], inc=s_brx)
                    p.op("vector", lambda e: e.tensor_tensor(out=rcr[:], in0=rp[:], in1=rge[:], op=ALU.mult),
                         waits=[(s_brx, s_brx.n)], inc=s_brx)
                    p.op("vector", lambda e: e.reduce_sum(out=rden[:], in_=rcr[:], axis=AX.X),
                         waits=[(s_brx, s_brx.n)], inc=s_brx)
                    p.op("vector", lambda e: e.reciprocal(out=rrc[:], in_=rden[:]),
                         waits=[(s_brx, s_brx.n)], inc=s_brx)
                    p.op("vector", lambda e: e.tensor_scalar(out=rcomb[:], in0=rcr[:], scalar1=rrc[:],
                                                             scalar2=None, op0=ALU.mult),
                         waits=[(s_brx, s_brx.n)], inc=s_brx)
                    p.op("tensor", lambda e: e.transpose(ps_rt[:], rcomb[:], ident[:]),
                         waits=[(s_brx, s_brx.n), (s_bcb, s_bcb.n)], inc=s_brt)
                    p.op("vector", lambda e, t=t: e.tensor_copy(out=combfm[:, t * 128:(t + 1) * 128], in_=ps_rt[:]),
                         waits=[(s_brt, s_brt.n)], inc=s_brx)
                n_comb = s_brx.n

                for tg in range(2):
                    def mmsg2(e, tg=tg):
                        ins = None
                        for c in range(NC):
                            ins = e.matmul(ps_m[0:1, 0:512], gwt[:, c:c + 1],
                                           x1f_v[:, c, tg * 512:(tg + 1) * 512],
                                           start=(c == 0), stop=(c == NC - 1))
                        return ins
                    p.op("tensor", mmsg2, waits=[(s_bx1c, n_x1t), (s_bsg, s_bsg.n), (s_brx, s_brx.n)], inc=s_brt)
                    p.op("scalar", lambda e, tg=tg: e.activation(out=sgrow[:, tg * 512:(tg + 1) * 512],
                                                                 in_=ps_m[0:1, 0:512], func=AF.Sigmoid),
                         waits=[(s_brt, s_brt.n)], inc=s_bsg)
                p.op("vector", lambda e: e.tensor_copy(out=sgrow_b[:], in_=sgrow[:]),
                     waits=[(s_bsg, s_bsg.n)], inc=s_bsg)
                for tg in range(2):
                    p.op("tensor", lambda e, tg=tg: e.matmul(ps_m[:, 0:512], ones1[:],
                                                             sgrow_b[:, tg * 512:(tg + 1) * 512],
                                                             start=True, stop=True),
                         waits=[(s_bsg, s_bsg.n), (s_brx, n_comb)], inc=s_brt)
                    p.op("scalar", lambda e, tg=tg: e.copy(out=sgb[:, tg * 512:(tg + 1) * 512], in_=ps_m[:, 0:512]),
                         waits=[(s_brt, s_brt.n), (s_busc, s_busc.n)], inc=s_bsg)
                n_sgb = s_bsg.n

                def upgate(i_local, gsrc_rows, usrc_rows, scale_sb, n_scale_ready, scale_sem):
                    slot = ug_ctr[0] % 2
                    ug_ctr[0] += 1
                    dma(p, "sync", gbuf[slot][:].rearrange("p (c f) -> p c f", c=NC), gsrc_rows, s_bwg,
                        waits=[(s_bg, wgu_user[slot]), (s_pc, 16 * NPREP)], transpose=True)
                    dma(p, "sync", ubuf[slot][:].rearrange("p (c f) -> p c f", c=NC), usrc_rows, s_bwg,
                        waits=[(s_bu, wgu_user[slot])], transpose=True)
                    nw = s_bwg.n
                    gb = gbuf[slot][:].rearrange("p (c f) -> p c f", c=NC)
                    ub = ubuf[slot][:].rearrange("p (c f) -> p c f", c=NC)
                    for tg in range(2):
                        def mmg(e, gb=gb, tg=tg):
                            ins = None
                            for c in range(NC):
                                ins = e.matmul(ps_g[:, tg * 512:(tg + 1) * 512], gb[:, c, :],
                                               x1f_v[:, c, tg * 512:(tg + 1) * 512],
                                               start=(c == 0), stop=(c == NC - 1))
                            return ins
                        p.op("tensor", mmg, waits=[(s_bwg, nw), (s_bx1c, n_x1t), (s_bsil, gh_user[tg])],
                             inc=s_bg)
                        p.op("scalar", lambda e, tg=tg: e.activation(
                            out=gs[:, tg * 512:(tg + 1) * 512], in_=ps_g[:, tg * 512:(tg + 1) * 512],
                            func=AF.Silu),
                            waits=[(s_bg, s_bg.n), (s_bgu, s_bgu.n)], inc=s_bsil)
                        gh_user[tg] = s_bsil.n
                    n_bg_here = s_bg.n
                    for tg in range(2):
                        def mmu(e, ub=ub, tg=tg):
                            ins = None
                            for c in range(NC):
                                ins = e.matmul(ps_u[:, tg * 512:(tg + 1) * 512], ub[:, c, :],
                                               x1f_v[:, c, tg * 512:(tg + 1) * 512],
                                               start=(c == 0), stop=(c == NC - 1))
                            return ins
                        p.op("tensor", mmu, waits=[(s_bwg, nw), (s_busc, uh_user[tg])], inc=s_bu)
                        p.op("vector", lambda e, tg=tg, sc=scale_sb: e.tensor_tensor(
                            out=usc[:, tg * 512:(tg + 1) * 512], in0=ps_u[:, tg * 512:(tg + 1) * 512],
                            in1=sc[:, tg * 512:(tg + 1) * 512], op=ALU.mult),
                            waits=[(s_bu, s_bu.n), (scale_sem, n_scale_ready), (s_bgu, s_bgu.n)], inc=s_busc)
                        uh_user[tg] = s_busc.n
                        p.op("gpsimd", lambda e, tg=tg, il=i_local: e.tensor_tensor(
                            out=sfm_v[:, il, tg * 512:(tg + 1) * 512],
                            in0=gs[:, tg * 512:(tg + 1) * 512], in1=usc[:, tg * 512:(tg + 1) * 512],
                            op=ALU.mult),
                            waits=[(s_bsil, s_bsil.n), (s_busc, s_busc.n), (s_bd, s_bd.n)], inc=s_bgu)
                    wgu_user[slot] = n_bg_here

                def downpass(nich, dsrc_rows_fn, first_contrib):
                    ngu = s_bgu.n
                    for fc in range(NC):
                        slot = dn_ctr[0] % 2
                        dn_ctr[0] += 1
                        dma(p, "sync", dbuf[slot][:, 0:nich * 128].rearrange("p (c f) -> p c f", c=nich),
                            dsrc_rows_fn(fc), s_bwd,
                            waits=[(s_bd, wd_user[slot]), (s_pc, 16 * NPREP)], transpose=True)
                        nwd = s_bwd.n
                        db = dbuf[slot][:, 0:nich * 128].rearrange("p (c f) -> p c f", c=nich)
                        for tg in range(2):
                            def mmd(e, db=db, tg=tg, nich=nich):
                                ins = None
                                for ic in range(nich):
                                    ins = e.matmul(ps_d[:, tg * 512:(tg + 1) * 512], db[:, ic, :],
                                                   sfm_v[:, ic, tg * 512:(tg + 1) * 512],
                                                   start=(ic == 0), stop=(ic == nich - 1))
                                return ins
                            p.op("tensor", mmd, waits=[(s_bwd, nwd), (s_bgu, ngu), (s_bacc, dh_user[tg])],
                                 inc=s_bd)
                            if first_contrib:
                                p.op("scalar", lambda e, fc=fc, tg=tg: e.copy(
                                    out=acc_v[:, fc, tg * 512:(tg + 1) * 512],
                                    in_=ps_d[:, tg * 512:(tg + 1) * 512]),
                                    waits=[(s_bd, s_bd.n), (s_bcast, s_bcast.n)], inc=s_bacc)
                            else:
                                p.op("vector", lambda e, fc=fc, tg=tg: e.tensor_tensor(
                                    out=acc_v[:, fc, tg * 512:(tg + 1) * 512],
                                    in0=ps_d[:, tg * 512:(tg + 1) * 512],
                                    in1=acc_v[:, fc, tg * 512:(tg + 1) * 512], op=ALU.add),
                                    waits=[(s_bd, s_bd.n)], inc=s_bacc)
                            dh_user[tg] = s_bacc.n
                        wd_user[slot] = s_bd.n

                for q4 in range(4):
                    for k8 in range(8):
                        ic = q4 * 8 + k8
                        upgate(k8, sgn[ic * 128:(ic + 1) * 128, :].rearrange("p (c f) -> p c f", c=NC),
                               sun[ic * 128:(ic + 1) * 128, :].rearrange("p (c f) -> p c f", c=NC),
                               sgb, n_sgb, s_bsg)
                    downpass(8, lambda fc, q4=q4: sdn[fc * 128:(fc + 1) * 128,
                                                      q4 * 1024:(q4 + 1) * 1024].rearrange("p (c f) -> p c f", c=8),
                             first_contrib=(q4 == 0))
                for e_ in range(E):
                    for tg in range(2):
                        p.op("tensor", lambda e, e_=e_, tg=tg: e.matmul(
                            ps_m[:, 0:512], sel[:, e_ * 128:(e_ + 1) * 128],
                            combfm[:, tg * 512:(tg + 1) * 512], start=True, stop=True),
                            waits=[(s_brx, n_comb), (s_bcb, s_bcb.n), (s_bsg, n_sgb), (s_tbl, 1)], inc=s_brt)
                        p.op("scalar", lambda e, tg=tg: e.copy(out=cbs[:, tg * 512:(tg + 1) * 512],
                                                               in_=ps_m[:, 0:512]),
                             waits=[(s_brt, s_brt.n), (s_busc, s_busc.n)], inc=s_bcb)
                    ncb = s_bcb.n
                    for ic in range(NIE):
                        r = (e_ * NIE + ic) * 128
                        upgate(ic, wgn[e_ * II + ic * 128: e_ * II + (ic + 1) * 128, :].rearrange("p (c f) -> p c f", c=NC),
                               wun[e_ * II + ic * 128: e_ * II + (ic + 1) * 128, :].rearrange("p (c f) -> p c f", c=NC),
                               cbs, ncb, s_bcb)
                    downpass(NIE, lambda fc, e_=e_: wdn[e_ * Dm + fc * 128: e_ * Dm + (fc + 1) * 128,
                                                        :].rearrange("p (c f) -> p c f", c=8),
                             first_contrib=False)

                x1tr = x1f[:].rearrange("p (tt c f) -> p tt c f", tt=8, c=NC)
                for t in range(8):
                    ms = t % 4
                    tok0 = off + t * 128
                    p.op("scalar", lambda e, t=t: e.copy(
                        out=x1tr[:, t, :, :],
                        in_=acc_v[:, :, t * 128:(t + 1) * 128]),
                        waits=[(s_bacc, s_bacc.n), (s_bg, s_bg.n), (s_bu, s_bu.n),
                               (s_brt, s_brt.n), (s_bgu, s_bgu.n)], inc=s_bcast)
                    bcast_tile[t] = s_bcast.n
                    dma(p, "sync", moetm[ms][:].rearrange("p (c f) -> p c f", c=NC),
                        x1tr[:, t, :, :], s_btr,
                        waits=[(s_bcast, s_bcast.n), (s_bpre, mtm_user[ms])], transpose=True)
                    n_tr = s_btr.n
                    btr_tile[t] = n_tr
                    dma(p, "sync", x1resstg[:], x1tm_s[seq, tok0:tok0 + 128, :], s_bx1r,
                        waits=[(s_bpre, s_bpre.n)])
                    p.op("vector", lambda e, ms=ms: e.tensor_tensor(out=pre2[:], in0=moetm[ms][:],
                                                                    in1=x1resstg[:], op=ALU.add),
                         waits=[(s_btr, n_tr), (s_bx1r, s_bx1r.n), (s_bx2, s_bx2.n)], inc=s_bpre)
                    mtm_user[ms] = s_bpre.n
                    p.op("vector", lambda e: e.reduce_sum(out=b_ns[:], in_=pre2[:], axis=AX.X, negate=True),
                         waits=[(s_bpre, s_bpre.n), (s_bt1, s_bt1.n)], inc=s_bns)
                    p.op("scalar", lambda e: e.activation(out=x2st[:], in_=pre2[:],
                                                          func=AF.Square, accum_out=b_sq[:]),
                         waits=[(s_bpre, s_bpre.n), (s_bt3, s_bt3.n), (s_out, s_out.n)])
                    p.op("scalar", lambda e: e.copy(out=b_sq[:], in_=b_sq[:]), waits=[], inc=s_bsq)
                    p.op("vector", lambda e: e.tensor_scalar(out=b_nm[:], in0=b_ns[:], scalar1=1.0 / Dm,
                                                             scalar2=None, op0=ALU.mult),
                         waits=[(s_bns, s_bns.n), (s_bx2, s_bx2.n)], inc=s_bt1)
                    p.op("vector", lambda e: e.tensor_tensor(out=b_m2[:], in0=b_nm[:], in1=b_nm[:], op=ALU.mult),
                         waits=[(s_bt1, s_bt1.n), (s_bt3, s_bt3.n)], inc=s_bt2)
                    p.op("vector", lambda e: e.tensor_scalar(out=b_var[:], in0=b_sq[:], scalar1=1.0 / Dm,
                                                             scalar2=b_m2[:], op0=ALU.mult, op1=ALU.subtract),
                         waits=[(s_bsq, s_bsq.n), (s_bt2, s_bt2.n), (s_bst_, s_bst_.n)], inc=s_bt3)
                    p.op("scalar", lambda e: e.activation(out=b_std[:], in_=b_var[:], func=AF.Sqrt, bias=eps_col[:]),
                         waits=[(s_bt3, s_bt3.n), (s_brs, s_brs.n)], inc=s_bst_)
                    p.op("vector", lambda e: e.reciprocal(out=b_rs[:], in_=b_std[:]),
                         waits=[(s_bst_, s_bst_.n), (s_bx2, s_bx2.n)], inc=s_brs)
                    p.op("vector", lambda e: e.tensor_scalar(out=x2st[:], in0=pre2[:], scalar1=b_nm[:],
                                                             scalar2=b_rs[:], op0=ALU.add, op1=ALU.mult),
                         waits=[(s_brs, s_brs.n), (s_bsq, s_bsq.n), (s_out, s_out.n)], inc=s_bx2)
                    dma(p, "sync", out[seq, tok0:tok0 + 128, :], x2st[:], s_out,
                        waits=[(s_bx2, s_bx2.n)])

            for blk in range(NBLK):
                stage_b(blk)

            if x1dump is not None:
                dma(p, "sync", x1dump[:, :, :], x1tm_s[:, :, :], s_out, waits=[(s_x1w, n_x1w_all)])
            n_out = s_out.n
            p.op("sync", lambda e: e.wait_ge(s_out.h, n_out), waits=[])

            with nc.Block() as block:
                def emit(eng_name):
                    def run(eng):
                        for fn, ws, tgt in p.q[eng_name]:
                            for s, v in ws:
                                eng.wait_ge(s.h, v)
                            ins = fn(eng)
                            if tgt is not None:
                                s, k = tgt
                                ins.then_inc(s.h, k)
                    return run

                block.tensor(emit("tensor"))
                block.vector(emit("vector"))
                block.scalar(emit("scalar"))
                block.gpsimd(emit("gpsimd"))
                block.sync(emit("sync"))

    return nc


def _host_tables():
    inv = np.power(10000.0, -2.0 * np.arange(HD // 2, dtype=np.float64) / HD)
    t = np.arange(T, dtype=np.float64)
    ang = t[None, :] * inv[:, None]
    cos = np.cos(ang)
    sin = np.sin(ang)
    cosT = np.concatenate([cos, cos], axis=0).astype(np.float32)
    sinT = np.concatenate([-sin, sin], axis=0).astype(np.float32)
    return cosT, sinT, (cosT * SCALE).astype(np.float32), (sinT * SCALE).astype(np.float32)


_NC_CACHE = {}


def kernel(L0_attn_x, L0_attn_Wq, L0_attn_Wk, L0_attn_Wv, L0_attn_Wo,
           L0_ffn_w_router, L0_ffn_wg, L0_ffn_wu, L0_ffn_wd,
           L0_ffn_sw_gate, L0_ffn_sw_up, L0_ffn_sw_down, L0_ffn_gate_w):
    x = np.ascontiguousarray(np.asarray(L0_attn_x, dtype=np.float32))
    cosT, sinT, cosQ, sinQ = _host_tables()
    if "nc" not in _NC_CACHE:
        _NC_CACHE["nc"] = build_program()
    nc = _NC_CACHE["nc"]
    common = {
        "wq": np.ascontiguousarray(np.asarray(L0_attn_Wq, np.float32)),
        "wk": np.ascontiguousarray(np.asarray(L0_attn_Wk, np.float32)),
        "wv": np.ascontiguousarray(np.asarray(L0_attn_Wv, np.float32)),
        "wo": np.ascontiguousarray(np.asarray(L0_attn_Wo, np.float32)),
        "wrouter": np.ascontiguousarray(np.asarray(L0_ffn_w_router, np.float32)),
        "wg": np.ascontiguousarray(np.asarray(L0_ffn_wg, np.float32).reshape(E * II, Dm)),
        "wu": np.ascontiguousarray(np.asarray(L0_ffn_wu, np.float32).reshape(E * II, Dm)),
        "wd": np.ascontiguousarray(np.asarray(L0_ffn_wd, np.float32).reshape(E * Dm, II)),
        "swg": np.ascontiguousarray(np.asarray(L0_ffn_sw_gate, np.float32)),
        "swu": np.ascontiguousarray(np.asarray(L0_ffn_sw_up, np.float32)),
        "swd": np.ascontiguousarray(np.asarray(L0_ffn_sw_down, np.float32)),
        "gatew": np.ascontiguousarray(np.asarray(L0_ffn_gate_w, np.float32)),
        "cosT": cosT,
        "sinT": sinT,
        "cosQ": cosQ,
        "sinQ": sinQ,
    }
    in_maps = []
    for c in range(NCORES):
        m = dict(common)
        m["x"] = x[c * SEQ:(c + 1) * SEQ]
        in_maps.append(m)
    res = run_bass_kernel_spmd(nc, in_maps, core_ids=list(range(NCORES)))
    return np.concatenate([np.asarray(r["out"], np.float32) for r in res.results], axis=0)
